# version 13
# baseline (speedup 1.0000x reference)
import numpy as np
from collections import deque
from contextlib import ExitStack

import concourse.bass as bass
import concourse.mybir as mybir
from concourse.bass_utils import run_bass_kernel_spmd

F32 = mybir.dt.float32
BF16 = mybir.dt.bfloat16
I16 = mybir.dt.int16
AF = mybir.ActivationFunctionType
ALU = mybir.AluOpType
AX = mybir.AxisListType

LAM = 0.01
EPS = 1e-5
KOFF = 128.0

ENGINES = ("pe", "act", "dve", "sp", "pool")


class Sched:

    NLANE = 12

    def __init__(self):
        self.sem_names = ["pe", "act", "dve", "sp", "pool"]
        for q in ("sp", "pool"):
            for l in range(self.NLANE):
                self.sem_names.append(f"{q}_l{l}")
        self.cnt = {sname: 0 for sname in self.sem_names}
        self.ops = {e: [] for e in ENGINES}
        self.lane_i = {"sp": 0, "pool": 0}
        self.writer = {}
        self.readers = {}
        self.water = {e: {sname: 0 for sname in self.sem_names} for e in ENGINES}
        self.wait_targets = {sname: set() for sname in self.sem_names}

    def op(self, eng, emit, reads=(), writes=(), dma=False, tag=""):
        need = {}

        def _need(psem, pcnt):
            if psem == eng and eng == "pe":
                return
            if self.water[eng][psem] < pcnt:
                need[psem] = max(need.get(psem, 0), pcnt)

        if dma:
            lane = self.lane_i[eng]
            self.lane_i[eng] = (lane + 1) % self.NLANE
            sname = f"{eng}_l{lane}"
            inc = 16
            if self.cnt[sname] > 0:
                _need(sname, self.cnt[sname])
        else:
            sname = eng
            inc = 1
        for k in reads:
            w = self.writer.get(k)
            if w:
                _need(*w)
        for k in writes:
            w = self.writer.get(k)
            if w:
                _need(*w)
            for psem, pcnt in self.readers.get(k, {}).items():
                _need(psem, pcnt)
        waits = []
        for psem, pcnt in need.items():
            waits.append((psem, pcnt))
            self.water[eng][psem] = pcnt
            self.wait_targets[psem].add(pcnt)
        self.cnt[sname] += inc
        mycnt = self.cnt[sname]
        self.ops[eng].append((waits, emit, inc, sname, tag))
        for k in writes:
            self.writer[k] = (sname, mycnt)
            self.readers[k] = {}
        for k in reads:
            self.readers.setdefault(k, {})
            self.readers[k][sname] = mycnt
        return mycnt

    def fence(self, eng, old_keys, new_keys, tag="fence"):
        self.op(eng, lambda e: e.nop(), reads=old_keys,
                writes=list(old_keys) + list(new_keys), tag=tag)

    def keys_with_names(self, names):
        ks = set()
        for k in list(self.writer.keys()) + list(self.readers.keys()):
            if isinstance(k, tuple) and len(k) > 0 and k[0] in names:
                ks.add(k)
        return sorted(ks, key=repr)

    def emit(self, block, sems):
        handles = {"pe": block.tensor, "act": block.scalar, "dve": block.vector,
                   "sp": block.sync, "pool": block.gpsimd}
        for ename in ENGINES:
            ops = self.ops[ename]
            if not ops:
                continue

            def make_body(ename=ename, ops=ops):
                def body(eng):
                    pending = {}
                    running = {}
                    last_close = {}
                    guarded = {}

                    def guard(sn):
                        v = last_close.get(sn, 0)
                        if v and not guarded.get(sn) and v in self.wait_targets[sn]:
                            eng.wait_ge(sems[sn], v)
                            guarded[sn] = True

                    n = len(ops)
                    for i, (waits, emitf, inc, sname, tag) in enumerate(ops):
                        for (psem, pcnt) in waits:
                            eng.wait_ge(sems[psem], pcnt)
                            if pcnt >= last_close.get(psem, 0):
                                guarded[psem] = True
                        will_close = (inc == 16
                                      or running.get(sname, 0) + inc
                                      in self.wait_targets[sname]
                                      or i == n - 1)
                        if will_close:
                            guard(sname)
                        inst = emitf(eng)
                        running[sname] = running.get(sname, 0) + inc
                        pending[sname] = pending.get(sname, 0) + inc
                        if will_close:
                            if inst is None:
                                inst = eng.nop()
                            inst.then_inc(sems[sname], pending[sname])
                            pending[sname] = 0
                            last_close[sname] = running[sname]
                            guarded[sname] = False
                    left = {sn: p for sn, p in pending.items() if p}
                    for sn, p in left.items():
                        guard(sn)
                        eng.nop().then_inc(sems[sn], p)
                return body
            handles[ename](make_body())


def build_program(cfg, taps=()):
    NSEQ, S, D, DL, INNER, H = (cfg["NSEQ"], cfg["S"], cfg["D"], cfg["DL"],
                                cfg["INNER"], cfg["H"])
    T = NSEQ * S
    NC = D // 128
    NCL = DL // 128
    NCI = INNER // 128
    NPAIR = H // 2
    SPAN = min(512, S)
    NSPAN = S // SPAN
    NKC = S // 128
    NFS = T // SPAN
    ISCALE = 1.0 / float(np.sqrt(D))
    V65W = H * 65

    nc = bass.Bass("TRN2", target_bir_lowering=False)
    sch = Sched()

    x_h = nc.dram_tensor("x", [T, D], F32, kind="ExternalInput").ap()
    wq_h = nc.dram_tensor("W_q", [D, D], F32, kind="ExternalInput").ap()
    wk_h = nc.dram_tensor("W_k", [D, D], F32, kind="ExternalInput").ap()
    wv_h = nc.dram_tensor("W_v", [D, D], F32, kind="ExternalInput").ap()
    wo_h = nc.dram_tensor("W_o", [D, D], F32, kind="ExternalInput").ap()
    wdkv_h = nc.dram_tensor("W_down_kv", [D, DL], F32, kind="ExternalInput").ap()
    wupk_h = nc.dram_tensor("W_up_k", [DL, D], F32, kind="ExternalInput").ap()
    wupv_h = nc.dram_tensor("W_up_v", [DL, D], F32, kind="ExternalInput").ap()
    wg_h = nc.dram_tensor("w_gate", [INNER, D], F32, kind="ExternalInput").ap()
    wu_h = nc.dram_tensor("w_up", [INNER, D], F32, kind="ExternalInput").ap()
    wd_h = nc.dram_tensor("w_down", [D, INNER], F32, kind="ExternalInput").ap()
    out_h = nc.dram_tensor("out", [T, D], F32, kind="ExternalOutput").ap()

    wgt_h = nc.dram_tensor("wg_t", [D, INNER], BF16, kind="Internal").ap()
    wut_h = nc.dram_tensor("wu_t", [D, INNER], BF16, kind="Internal").ap()
    wdt_h = nc.dram_tensor("wd_t", [INNER, D], BF16, kind="Internal").ap()
    zdram_h = nc.dram_tensor("zdram", [H, SPAN], BF16, kind="Internal").ap()
    zrd_h = nc.dram_tensor("zrd", [H, SPAN], BF16, kind="Internal").ap()
    gdram_h = nc.dram_tensor("gdram", [4, 2048], F32, kind="Internal").ap()
    x1d_h = nc.dram_tensor("x1dram", [128, NSEQ * NC * S], BF16, kind="Internal").ap()
    DBG = cfg.get("DBG", False)
    if DBG:
        dbgA_h = nc.dram_tensor("dbgA", [128, 512], BF16, kind="ExternalOutput").ap()
        dbgB_h = nc.dram_tensor("dbgB", [128, 512], BF16, kind="ExternalOutput").ap()
        dbgC_h = nc.dram_tensor("dbgC", [128, 512], BF16, kind="ExternalOutput").ap()
        dbgD_h = nc.dram_tensor("dbgD", [128, 512], F32, kind="ExternalOutput").ap()
        dbgE_h = nc.dram_tensor("dbgE", [128, 512], BF16, kind="ExternalOutput").ap()
        dbgF_h = nc.dram_tensor("dbgF", [128, 512], F32, kind="ExternalOutput").ap()

    tap_h = {}
    for tname, pdim, fdim, tdt in taps:
        tap_h[tname] = nc.dram_tensor("tap_" + tname, [pdim, fdim], tdt,
                                      kind="ExternalOutput").ap()

    ctx = ExitStack()

    def sb(name, shape, dt, c=None):
        return (c or ctx).enter_context(nc.sbuf_tensor(name, shape, dt))

    def psalloc(name, shape, dt):
        return ctx.enter_context(nc.psum_tensor(name, shape, dt))

    cD = sb("cD", [128, S], BF16)
    cE = sb("cE", [128, S], BF16)
    cC = sb("cC", [128, 128], BF16)
    cId = sb("cId", [128, 128], BF16)
    ecolE = sb("ecolE", [128, NKC], F32)
    ecolD = sb("ecolD", [128, NKC], F32)
    ecol2 = sb("ecol2", [128, NKC], F32)
    cE2 = sb("cE2", [128, S], BF16)
    ones_bf = sb("ones_bf", [128, 1], BF16)
    onesrow = sb("onesrow", [1, 128], BF16)
    onesrow_f = sb("onesrow_f", [1, 128], F32)
    onescol_f = sb("onescol_f", [128, 1], F32)
    pairsel = sb("pairsel", [2, 128], BF16)
    wfa = sb("wfa", [128, 512], F32)
    wfb = sb("wfb", [128, 512], F32)
    wba = sb("wba", [128, 512], BF16)
    wbb = sb("wbb", [128, 512], BF16)
    wbc = sb("wbc", [128, 512], BF16)
    wbd = sb("wbd", [128, 512], BF16)
    wpa = sb("wpa", [128, 512], F32)
    wpb = sb("wpb", [128, 512], F32)
    wta = sb("wta", [128, 512], BF16)
    wtb = sb("wtb", [128, 512], BF16)
    wqA = sb("wqA", [128, 512], BF16)
    wqB = sb("wqB", [128, 512], BF16)
    gA = wta
    gB = wtb

    xstage = sb("xstage", [128, 512], F32)
    xstgb = sb("xstgb", [128, 512], BF16)
    scrA = sb("scrA", [128, 512], F32)
    scrI = sb("scrI", [128, 512], I16)
    scc = sb("scc", [128, 16], F32)
    r1 = sb("r1", [1, 512], F32)
    r2 = sb("r2", [1, 512], F32)
    rb1 = sb("rb1", [1, 512], BF16)
    rb2 = sb("rb2", [1, 512], BF16)
    bcA = sb("bcA", [128, 512], BF16)
    bcB = sb("bcB", [128, 512], BF16)
    thrcol = sb("thrcol", [128, 6], F32)
    mgcol = sb("mgcol", [128, 1], F32)
    colsumK = sb("colsumK", [128, NC], F32)
    zinb = sb("zinb", [128, (H * SPAN) // 128], BF16)
    epsc = sb("epsc", [1, 1], F32)
    zoutf = sb("zoutf", [128, (H * SPAN) // 128], BF16)
    zprb = sb("zprb", [2, SPAN], BF16)

    psP1 = psalloc("psP1", [128, 1024], F32)
    psP2 = psalloc("psP2", [128, 1024], F32)
    psQ1 = psalloc("psQ1", [128, 512], F32)
    psQ2 = psalloc("psQ2", [128, 512], F32)
    psM = psalloc("psM", [128, 512], F32)
    psB = psalloc("psB", [128, 1024], BF16)

    sems = {}

    def K(*a):
        return a

    def dve(emit, reads=(), writes=(), tag=""):
        sch.op("dve", emit, reads, writes, tag=tag)

    def act(emit, reads=(), writes=(), tag=""):
        sch.op("act", emit, reads, writes, tag=tag)

    def pe(emit, reads=(), writes=(), tag=""):
        sch.op("pe", emit, reads, writes, tag=tag)

    def dma(q, emit, reads=(), writes=(), tag=""):
        sch.op(q, emit, reads, writes, dma=True, tag=tag)

    def pool(emit, reads=(), writes=(), tag=""):
        sch.op("pool", emit, reads, writes, tag=tag)

    PS4 = [(psP1, 0, K("ps4", 0)), (psP1, 512, K("ps4", 1)),
           (psP2, 0, K("ps4", 2)), (psP2, 512, K("ps4", 3))]
    PK_P1 = [K("ps4", 0), K("ps4", 1)]
    PK_P2 = [K("ps4", 2), K("ps4", 3)]

    for off in range(0, S, 512):
        n = min(512, S - off)
        pool(lambda g, off=off, n=n: g.iota(
            scrA[:, 0:n], [[1, n]], base=off, channel_multiplier=0,
            allow_small_or_imprecise_dtypes=True),
            reads=[K("scrA")], writes=[K("scrA")], tag="iota")
        act(lambda s, off=off, n=n: s.activation(
            cD[:, off:off + n], scrA[:, 0:n], AF.Exp, scale=-LAM),
            reads=[K("scrA")], writes=[K("cD")], tag="mkD")
        act(lambda s, off=off, n=n: s.activation(
            cE[:, off:off + n], scrA[:, 0:n], AF.Exp, scale=LAM),
            reads=[K("scrA")], writes=[K("cE")], tag="mkE")
    for off in range(0, S, 512):
        n = min(512, S - off)
        pool(lambda g, off=off, n=n: g.iota(
            scrA[:, 0:n], [[1, n]], base=off, channel_multiplier=0,
            allow_small_or_imprecise_dtypes=True),
            reads=[K("scrA")], writes=[K("scrA")], tag="iota2")
        act(lambda s, off=off, n=n: s.activation(
            cE2[:, off:off + n], scrA[:, 0:n], AF.Exp, scale=2.0 * LAM),
            reads=[K("scrA")], writes=[K("cE2")], tag="mkE2")
    dve(lambda v: v.tensor_scalar(cD[:, :], cD[:, :], ISCALE, None, ALU.mult),
        reads=[K("cD")], writes=[K("cD")], tag="cDs")
    dve(lambda v: v.tensor_scalar(cE[:, :], cE[:, :], ISCALE, None, ALU.mult),
        reads=[K("cE")], writes=[K("cE")], tag="cEs")
    pool(lambda g: g.iota(scrA[:, 128:256], [[-1, 128]], base=0,
                          channel_multiplier=1,
                          allow_small_or_imprecise_dtypes=True),
         reads=[K("scrA")], writes=[K("scrA")], tag="delta")
    act(lambda s: s.activation(scrA[:, 256:384], scrA[:, 128:256], AF.Relu),
        reads=[K("scrA")], writes=[K("scrA")], tag="reluD")
    act(lambda s: s.activation(cC[:, :], scrA[:, 256:384], AF.Exp, scale=-2.0 * LAM),
        reads=[K("scrA")], writes=[K("cC")], tag="mkC")
    dve(lambda v: v.tensor_scalar(cId[:, :], scrA[:, 128:256], 0.0, None,
                                  ALU.is_equal),
        reads=[K("scrA")], writes=[K("cId")], tag="mkId")
    pool(lambda g: g.iota(scrA[:, 384:385], [[0, 1]], base=0, channel_multiplier=1,
                          allow_small_or_imprecise_dtypes=True),
         reads=[K("scrA")], writes=[K("scrA")], tag="pcol")
    act(lambda s: s.activation(scrA[:, 385:386], scrA[:, 384:385], AF.Exp,
                               scale=LAM),
        reads=[K("scrA")], writes=[K("scrA")], tag="ebE")
    act(lambda s: s.activation(scrA[:, 386:387], scrA[:, 384:385], AF.Exp,
                               scale=-LAM),
        reads=[K("scrA")], writes=[K("scrA")], tag="ebD")
    for kc in range(NKC):
        fE = float(np.exp(LAM * 128.0 * kc))
        fD = float(np.exp(-LAM * 128.0 * kc))
        dve(lambda v, kc=kc, fE=fE: v.tensor_scalar(
            ecolE[:, kc:kc + 1], scrA[:, 385:386], fE, None, ALU.mult),
            reads=[K("scrA")], writes=[K("ecolE")], tag="ecE")
        dve(lambda v, kc=kc, fD=fD: v.tensor_scalar(
            ecolD[:, kc:kc + 1], scrA[:, 386:387], fD, None, ALU.mult),
            reads=[K("scrA")], writes=[K("ecolD")], tag="ecD")
        dve(lambda v, kc=kc, fD=fD: v.tensor_scalar(
            ecol2[:, kc:kc + 1], ecolD[:, kc:kc + 1], ecolD[:, kc:kc + 1], None,
            ALU.mult),
            reads=[K("ecolD")], writes=[K("ecol2")], tag="ec2")
    dve(lambda v: v.memset(ones_bf[:, :], 1.0), writes=[K("ones_bf")], tag="c1")
    dve(lambda v: v.memset(epsc[:, :], EPS), writes=[K("epsc")], tag="ceps")
    dve(lambda v: v.memset(onesrow[:, :], 1.0), writes=[K("onesrow")], tag="c2")
    dve(lambda v: v.memset(onesrow_f[:, :], 1.0), writes=[K("onesrow_f")], tag="c3")
    dve(lambda v: v.memset(onescol_f[:, :], 1.0), writes=[K("onescol_f")], tag="c4")
    pool(lambda g: g.iota(scrA[0:2, 0:128], [[1, 128]], base=0,
                          channel_multiplier=-64,
                          allow_small_or_imprecise_dtypes=True),
         reads=[K("scrA")], writes=[K("scrA")], tag="c5i")
    dve(lambda v: v.tensor_scalar(scrA[0:2, 128:256], scrA[0:2, 0:128], 0.0, None,
                                  ALU.is_ge),
        reads=[K("scrA")], writes=[K("scrA")], tag="c5a")
    dve(lambda v: v.tensor_scalar(scrA[0:2, 0:128], scrA[0:2, 0:128], 63.0, None,
                                  ALU.is_le),
        reads=[K("scrA")], writes=[K("scrA")], tag="c5b")
    dve(lambda v: v.tensor_tensor(pairsel[:, :], scrA[0:2, 0:128],
                                  scrA[0:2, 128:256], ALU.mult),
        reads=[K("scrA")], writes=[K("pairsel")], tag="c5c")

    _sf = [0]
    _sbp = [0]
    _pf = [0]
    _pb = [0]

    def wbuf_f():
        i = _sf[0]; _sf[0] += 1
        return ((wfa, K("wfa")), (wfb, K("wfb")))[i % 2]

    def wbuf_b():
        i = _sbp[0]; _sbp[0] += 1
        return ((wba, K("wba")), (wbb, K("wbb")), (wbc, K("wbc")),
                (wbd, K("wbd")))[i % 4]

    PPF = deque([(wpa, K("wpa")), (wpb, K("wpb"))])
    PPB = [(wta, K("wta")), (wtb, K("wtb"))]
    PPS = [(wqB, K("wqB"))]
    _ps = [0]

    def pbuf_b():
        i = _pb[0]; _pb[0] += 1
        return PPB[i % len(PPB)]

    def pbuf_s():
        i = _ps[0]; _ps[0] += 1
        return PPS[i % len(PPS)]

    def wconvert(src_h, dst_h, rows, cols, q, extra=None, name="wc"):
        nch = rows // 128
        cw = min(512, cols)
        nst = cols // cw
        for i in range(nch):
            for j in range(nst):
                buf, bk = wbuf_f()
                bb, bbk = wbuf_b()
                dma(q, lambda e, i=i, j=j, buf=buf, src_h=src_h, cw=cw:
                    e.dma_start(buf[:, 0:cw],
                                src_h[128 * i:128 * i + 128, cw * j:cw * j + cw]),
                    reads=[bk], writes=[bk], tag=f"{name}ld")
                if extra is not None:
                    dma(q, lambda e, i=i, j=j, cw=cw, extra=extra:
                        e.dma_start(scrA[:, 0:cw],
                                    extra[128 * i:128 * i + 128, cw * j:cw * j + cw]),
                        reads=[K("scrA")], writes=[K("scrA")], tag=f"{name}l2")
                    dve(lambda v, buf=buf, bb=bb, cw=cw: v.tensor_tensor(
                        bb[:, 0:cw], buf[:, 0:cw], scrA[:, 0:cw], ALU.add),
                        reads=[bk, K("scrA")], writes=[bbk], tag=f"{name}ad")
                else:
                    dve(lambda v, buf=buf, bb=bb, cw=cw: v.tensor_copy(
                        bb[:, 0:cw], buf[:, 0:cw]),
                        reads=[bk], writes=[bbk], tag=f"{name}cs")
                dma(q, lambda e, i=i, j=j, bb=bb, dst_h=dst_h, cw=cw:
                    e.dma_start(dst_h[128 * i:128 * i + 128, cw * j:cw * j + cw],
                                bb[:, 0:cw]),
                    reads=[bbk], writes=[K(name, i, j)], tag=f"{name}st")

    def wstats(src_h, rows, cols, slot, want_m, q, name):
        nch = rows // 128
        nst = cols // 512
        first = [True]
        for i in range(nch):
            for j in range(nst):
                yield
                buf, bk = PPF.popleft()
                dma(q, lambda e, i=i, j=j, buf=buf, src_h=src_h:
                    e.dma_start(buf[:, :], src_h[128 * i:128 * i + 128,
                                                 512 * j:512 * j + 512]),
                    reads=[bk], writes=[bk], tag=f"{name}sl")
                if first[0]:
                    dve(lambda v, buf=buf: v.tensor_reduce(
                        scc[:, 0:1], buf[:, :], AX.X, ALU.add,
                        apply_absolute_value=True),
                        reads=[bk, K("scc0")], writes=[K("scc0")], tag=f"{name}r0")
                    first[0] = False
                else:
                    dve(lambda v, buf=buf: v.tensor_reduce(
                        scc[:, 1:2], buf[:, :], AX.X, ALU.add,
                        apply_absolute_value=True),
                        reads=[bk, K("scc1")], writes=[K("scc1")], tag=f"{name}r1")
                    dve(lambda v: v.tensor_tensor(scc[:, 0:1], scc[:, 0:1],
                                                  scc[:, 1:2], ALU.add),
                        reads=[K("scc0"), K("scc1")], writes=[K("scc0")],
                        tag=f"{name}ra")
                PPF.append((buf, bk))
        pe(lambda t: t.matmul(psM[0:1, 0:1], onescol_f[:, :], scc[:, 0:1],
                              start=True, stop=True),
           reads=[K("onescol_f"), K("scc0")], writes=[K("psM")], tag=f"{name}tm")
        nel = float(rows * cols)
        dve(lambda v: v.tensor_scalar(r1[0:1, 0:1], psM[0:1, 0:1], 1.0 / nel, 1e-5,
                                      ALU.mult, ALU.max),
            reads=[K("psM"), K("r1")], writes=[K("r1")], tag=f"{name}mn")
        dve(lambda v: v.tensor_scalar(r1[0:1, 1:2], r1[0:1, 0:1], 0.5, None, ALU.mult),
            reads=[K("r1")], writes=[K("r1")], tag=f"{name}th")
        pe(lambda t: t.matmul(psM[0:128, 1:2], onesrow_f[:, :], r1[0:1, 1:2],
                              start=True, stop=True),
           reads=[K("onesrow_f"), K("r1")], writes=[K("psM")], tag=f"{name}bm")
        dve(lambda v, slot=slot: v.tensor_copy(thrcol[:, slot:slot + 1], psM[:, 1:2]),
            reads=[K("psM"), K("thr")], writes=[K("thr")], tag=f"{name}tc")
        dve(lambda v, slot=slot: v.tensor_scalar(
            thrcol[:, slot + 1:slot + 2], thrcol[:, slot:slot + 1], -1.0, None,
            ALU.mult),
            reads=[K("thr")], writes=[K("thr")], tag=f"{name}nt")
        if want_m:
            pe(lambda t: t.matmul(psM[0:128, 2:3], onesrow_f[:, :], r1[0:1, 0:1],
                                  start=True, stop=True),
               reads=[K("onesrow_f"), K("r1")], writes=[K("psM")], tag=f"{name}m2")
            dve(lambda v: v.tensor_copy(mgcol[:, :], psM[:, 2:3]),
                reads=[K("psM")], writes=[K("mgcol")], tag=f"{name}mc")

    def wquant(src_h, dst_h, rows, cols, slot, q, name, colsum=False):
        nch = rows // 128
        nst = cols // 512
        idx = [(i, j) for i in range(nch) for j in range(nst)]
        pend = deque()
        li = [0]

        def emit_load():
            if li[0] < len(idx) and PPF:
                i, j = idx[li[0]]
                li[0] += 1
                buf, bk = PPF.popleft()
                dma(q, lambda e, i=i, j=j, buf=buf, src_h=src_h:
                    e.dma_start(buf[:, :], src_h[128 * i:128 * i + 128,
                                                 512 * j:512 * j + 512]),
                    reads=[bk], writes=[bk], tag=f"{name}ql")
                pend.append((i, j, buf, bk))

        for _step in range(len(idx)):
            yield
            while li[0] < len(idx) and PPF:
                emit_load()
            i, j, buf, bk = pend.popleft()
            tb, tbk = pbuf_b()
            dve(lambda v, buf=buf, slot=slot: v.tensor_scalar(
                wqA[:, :], buf[:, :], thrcol[:, slot:slot + 1], None, ALU.is_ge),
                reads=[bk, K("thr"), K("wqA")], writes=[K("wqA")],
                tag=f"{name}ge")
            dve(lambda v, buf=buf, slot=slot: v.tensor_scalar(
                wqB[:, :], buf[:, :], thrcol[:, slot + 1:slot + 2], None,
                ALU.is_le),
                reads=[bk, K("thr"), K("wqB")], writes=[K("wqB")],
                tag=f"{name}le")
            PPF.append((buf, bk))
            dve(lambda v, tb=tb: v.tensor_tensor(tb[:, :], wqA[:, :],
                                                 wqB[:, :], ALU.subtract),
                reads=[K("wqA"), K("wqB")], writes=[tbk], tag=f"{name}tn")
            if colsum:
                dve(lambda v, tb=tb: v.tensor_reduce(scc[:, 2:3], tb[:, :],
                                                     AX.X, ALU.add),
                    reads=[tbk, K("scc2")], writes=[K("scc2")], tag=f"{name}cs")
                if j == 0:
                    dve(lambda v, i=i: v.tensor_copy(colsumK[:, i:i + 1],
                                                     scc[:, 2:3]),
                        reads=[K("scc2"), K("csK")], writes=[K("csK")],
                        tag=f"{name}c0")
                else:
                    dve(lambda v, i=i: v.tensor_tensor(
                        colsumK[:, i:i + 1], colsumK[:, i:i + 1], scc[:, 2:3],
                        ALU.add),
                        reads=[K("scc2"), K("csK")], writes=[K("csK")],
                        tag=f"{name}ca")
            for b in range(4):
                pe(lambda t, tb=tb, b=b: t.transpose(
                    psB[:, 512 + 128 * b:512 + 128 * b + 128],
                    tb[:, 128 * b:128 * b + 128], cId[:, :]),
                   reads=[tbk, K("cId")], writes=[K("psB2")], tag=f"{name}tr")
            stg, sk = pbuf_s()
            act(lambda s, stg=stg: s.activation(stg[:, 0:512],
                                                psB[:, 512:1024], AF.Copy),
                reads=[K("psB2"), sk], writes=[sk], tag=f"{name}ev")

            def emst(e, i=i, j=j, stg=stg):
                d4 = dst_h.rearrange("(a p) c -> a p c", p=128)
                dd = d4[4 * j:4 * j + 4, :, 128 * i:128 * i + 128]
                dd = dd.rearrange("a p o -> p a o")
                src = stg[:, 0:512].rearrange("p (a o) -> p a o", o=128)
                return e.dma_start(dd, src)
            dma(q, emst, reads=[sk], writes=[K(name, "q", i, j)],
                tag=f"{name}qs")

    def wq_strip_keys(name, ci, og4):
        return [K(name, "q", 4 * og4 + r, ci // 4) for r in range(4)]

    class Kit:
        def __init__(self, **kw):
            self.__dict__.update(kw)

    KIT0 = Kit(psS=psM, psSk=K("psM"), psQ=psQ1, psQk=K("psQ1"), single=False,
               r1=r1, r1k=K("r1"), r2=r2, r2k=K("r2"),
               rb1=rb1, rb1k=K("rb1"), rb2=rb2, rb2k=K("rb2"),
               bcA=bcA, bcAk=K("bcA"), bcB=bcB, bcBk=K("bcB"),
               xs=xstgb, xsk=K("xstgb"), m2=scrA, m2k=K("scrA"))

    def ln_stats(src_ap_fn, src_keys, uid, kt=None):
        kt = kt or KIT0
        ln_stats_a(src_ap_fn, src_keys, uid, kt)
        ln_stats_b(uid, kt)

    def ln_stats_a(src_ap_fn, src_keys, uid, kt=None):
        kt = kt or KIT0
        for c in range(NC):
            pe(lambda t, c=c: t.matmul(kt.psS[0:1, 0:SPAN], ones_bf[:, :],
                                       src_ap_fn(c), start=(c == 0),
                                       stop=(c == NC - 1)),
               reads=[K("ones_bf")] + src_keys(c), writes=[kt.psSk],
               tag=f"{uid}sm")
        if kt.single:
            dve(lambda v: v.tensor_scalar(kt.r1[0:1, 0:SPAN],
                                          kt.psS[0:1, 0:SPAN], 1.0 / D,
                                          None, ALU.mult),
                reads=[kt.psSk, kt.r1k], writes=[kt.r1k], tag=f"{uid}m")
        for c in range(NC):
            act(lambda s, c=c: s.activation(kt.xs[:, 0:SPAN], src_ap_fn(c),
                                            AF.Square),
                reads=src_keys(c) + [kt.xsk], writes=[kt.xsk],
                tag=f"{uid}sq")
            pe(lambda t, c=c: t.matmul(kt.psQ[0:1, 0:SPAN], ones_bf[:, :],
                                       kt.xs[:, 0:SPAN], start=(c == 0),
                                       stop=(c == NC - 1)),
               reads=[K("ones_bf"), kt.xsk] + ([kt.r1k] if kt.single else []),
               writes=[kt.psQk],
               tag=f"{uid}ss")
        if not kt.single:
            dve(lambda v: v.tensor_scalar(kt.r1[0:1, 0:SPAN],
                                          kt.psS[0:1, 0:SPAN], 1.0 / D,
                                          None, ALU.mult),
                reads=[kt.psSk, kt.r1k], writes=[kt.r1k], tag=f"{uid}m")
        dve(lambda v: v.tensor_scalar(kt.r2[0:1, 0:SPAN], kt.psQ[0:1, 0:SPAN],
                                      1.0 / D, None, ALU.mult),
            reads=[kt.psQk, kt.r2k], writes=[kt.r2k], tag=f"{uid}q")
        dve(lambda v: v.tensor_tensor(kt.m2[0:1, 0:SPAN], kt.r1[0:1, 0:SPAN],
                                      kt.r1[0:1, 0:SPAN], ALU.mult),
            reads=[kt.r1k, kt.m2k], writes=[kt.m2k], tag=f"{uid}m2")
        dve(lambda v: v.tensor_tensor(kt.r2[0:1, 0:SPAN], kt.r2[0:1, 0:SPAN],
                                      kt.m2[0:1, 0:SPAN], ALU.subtract),
            reads=[kt.r2k, kt.m2k], writes=[kt.r2k], tag=f"{uid}vr")
        act(lambda s: s.activation(kt.r2[0:1, 0:SPAN], kt.r2[0:1, 0:SPAN],
                                   AF.Ln, bias=epsc[0:1, 0:1]),
            reads=[kt.r2k, K("epsc")], writes=[kt.r2k], tag=f"{uid}ln")
        act(lambda s: s.activation(kt.r2[0:1, 0:SPAN], kt.r2[0:1, 0:SPAN],
                                   AF.Exp, scale=-0.5),
            reads=[kt.r2k], writes=[kt.r2k], tag=f"{uid}rs")
        dve(lambda v: v.tensor_tensor(kt.r1[0:1, 0:SPAN], kt.r1[0:1, 0:SPAN],
                                      kt.r2[0:1, 0:SPAN], ALU.mult),
            reads=[kt.r1k, kt.r2k], writes=[kt.r1k], tag=f"{uid}mr")

    def ln_stats_b(uid, kt=None):
        kt = kt or KIT0
        pe(lambda t: t.matmul(kt.psS[:, 0:SPAN], onesrow_f[:, :],
                              kt.r2[0:1, 0:SPAN], start=True, stop=True),
           reads=[K("onesrow_f"), kt.r2k], writes=[kt.psSk], tag=f"{uid}b1")
        act(lambda a: a.activation(kt.bcA[:, 0:SPAN], kt.psS[:, 0:SPAN],
                                   AF.Copy),
            reads=[kt.psSk, kt.bcAk], writes=[kt.bcAk], tag=f"{uid}e1")
        pe(lambda t: t.matmul(kt.psS[:, 0:SPAN], onesrow_f[:, :],
                              kt.r1[0:1, 0:SPAN], start=True, stop=True),
           reads=[K("onesrow_f"), kt.r1k], writes=[kt.psSk], tag=f"{uid}b2")
        act(lambda a: a.activation(kt.bcB[:, 0:SPAN], kt.psS[:, 0:SPAN],
                                   AF.Copy),
            reads=[kt.psSk, kt.bcBk], writes=[kt.bcBk], tag=f"{uid}e2")

    def _prep_gen():
        yield from wstats(wg_h, INNER, D, 0, True, "pool", "wg")
        yield from wstats(wu_h, INNER, D, 2, False, "pool", "wu")
        yield from wstats(wd_h, D, INNER, 4, False, "pool", "wd")
        yield from wquant(wg_h, wgt_h, INNER, D, 0, "pool", "wg")
        yield from wquant(wu_h, wut_h, INNER, D, 2, "pool", "wu")
        yield from wquant(wd_h, wdt_h, D, INNER, 4, "pool", "wd", colsum=True)
        dve(lambda v: v.tensor_scalar(colsumK[:, 0:NC], colsumK[:, 0:NC], KOFF,
                                      None, ALU.mult),
            reads=[K("csK")], writes=[K("csK")], tag="csKs")
    _prep = _prep_gen()

    def pump(n=2):
        for _ in range(n):
            if next(_prep, "done") == "done":
                break

    seq_ctx = ExitStack()
    x1 = sb("x1sb", [128, NC * S], BF16, seq_ctx)
    q_sb = sb("q_sb", [128, NC * S], BF16, seq_ctx)
    krec_sb = sb("krec_sb", [128, NC * S], BF16, seq_ctx)
    v65 = sb("v65", [128, NKC * V65W], BF16, seq_ctx)
    ao0 = sb("ao0", [128, NC * SPAN], BF16, seq_ctx)
    ao1 = sb("ao1", [128, NC * SPAN], BF16, seq_ctx)
    zst = sb("zst", [65, SPAN], BF16, seq_ctx)
    zst2 = sb("zst2", [65, SPAN], BF16, seq_ctx)
    LATK = ["lat", "xln"]
    SCOREK = ["qsl", "pp", "pj"]

    def x1ap(c, lt, n):
        return x1[:, c * S + lt: c * S + lt + n]

    def x1k(c, lt):
        return K("x1", c, lt // SPAN)

    def wstrip_mm(wdram, wdram_keys, NI, NO, rhs_fn, rhs_keys, uid, evac_fn,
                  strip_w=512, cast=False, extra=None):
        npg = strip_w // 128
        for og4 in range(NO // npg):
            for ci in range(NI):
                wb, wk = wbuf_b()
                if cast:
                    f1, fk1 = wbuf_f()
                    dma("sp", lambda e, og4=og4, ci=ci, f1=f1, wdram=wdram,
                        strip_w=strip_w:
                        e.dma_start(f1[:, 0:strip_w],
                                    wdram[128 * ci:128 * ci + 128,
                                          strip_w * og4:strip_w * og4 + strip_w]),
                        reads=[fk1] + wdram_keys(ci, og4), writes=[fk1],
                        tag=f"{uid}wl")
                    if extra is not None:
                        f2, fk2 = wbuf_f()
                        dma("sp", lambda e, og4=og4, ci=ci, f2=f2, extra=extra,
                            strip_w=strip_w:
                            e.dma_start(f2[:, 0:strip_w],
                                        extra[128 * ci:128 * ci + 128,
                                              strip_w * og4:strip_w * og4 + strip_w]),
                            reads=[fk2], writes=[fk2], tag=f"{uid}w2")
                        dve(lambda v, f1=f1, f2=f2, wb=wb, strip_w=strip_w:
                            v.tensor_tensor(wb[:, 0:strip_w], f1[:, 0:strip_w],
                                            f2[:, 0:strip_w], ALU.add),
                            reads=[fk1, fk2, wk], writes=[wk], tag=f"{uid}wa")
                    else:
                        act(lambda a, f1=f1, wb=wb, strip_w=strip_w:
                            a.activation(wb[:, 0:strip_w], f1[:, 0:strip_w],
                                         AF.Copy),
                            reads=[fk1, wk], writes=[wk], tag=f"{uid}wc")
                else:
                    dma("sp", lambda e, og4=og4, ci=ci, wb=wb, wdram=wdram,
                        strip_w=strip_w:
                        e.dma_start(wb[:, 0:strip_w],
                                    wdram[128 * ci:128 * ci + 128,
                                          strip_w * og4:strip_w * og4 + strip_w]),
                        reads=[wk] + wdram_keys(ci, og4), writes=[wk],
                        tag=f"{uid}wl")
                for jj in range(npg):
                    pt, poff, pk = PS4[jj]
                    pe(lambda t, ci=ci, jj=jj, wb=wb, pt=pt, poff=poff:
                       t.matmul(pt[0:128, poff:poff + SPAN],
                                wb[:, 128 * jj:128 * jj + 128], rhs_fn(ci),
                                start=(ci == 0), stop=(ci == NI - 1)),
                       reads=[wk] + rhs_keys(ci), writes=[pk], tag=f"{uid}mm")
            for jj in range(npg):
                evac_fn(og4 * npg + jj, jj)

    def seq_run(s_i):
        t0 = s_i * S
        phase_ctx = ExitStack()
        lat_sb = sb("lat_sb%d" % s_i, [128, NCL * S], BF16, phase_ctx)
        xln = sb("xln%d" % s_i, [128, NC * SPAN], BF16, phase_ctx)
        sch.fence("dve", sch.keys_with_names(SCOREK),
                  [K("lat", c, sp) for c in range(NCL) for sp in range(NSPAN)] +
                  [K("xln", c) for c in range(NC)],
                  tag="fence_lat")

        for tt in range(S // 128):
            tg = t0 + tt * 128
            lt = tt * 128
            for g in range(2):
                dma("sp", lambda e, tg=tg, g=g: e.dma_start(
                    xstage[:, 0:512], x_h[tg:tg + 128, 512 * g:512 * g + 512]),
                    reads=[K("xstage")], writes=[K("xstage")], tag="xld")
                act(lambda a: a.activation(xstgb[:, 0:512], xstage[:, 0:512],
                                           AF.Copy),
                    reads=[K("xstage"), K("xstgb")], writes=[K("xstgb")], tag="xcs")
                for b in range(4):
                    pe(lambda t, b=b: t.transpose(
                        psB[:, 128 * b:128 * b + 128],
                        xstgb[:, 128 * b:128 * b + 128], cId[:, :]),
                       reads=[K("xstgb"), K("cId")], writes=[K("psB")], tag="xtr")
                for b in range(4):
                    c = 4 * g + b
                    dve(lambda v, c=c, b=b, lt=lt: v.tensor_copy(
                        x1ap(c, lt, 128), psB[:, 128 * b:128 * b + 128]),
                        reads=[K("psB")], writes=[x1k(c, lt)], tag="xev")

        for sp in range(NSPAN):
            lt0 = sp * SPAN
            ln_stats(lambda c, lt0=lt0: x1ap(c, lt0, SPAN),
                     lambda c, lt0=lt0: [x1k(c, lt0)], f"l1_{s_i}_{sp}_")
            for c in range(NC):
                dve(lambda v, c=c, lt0=lt0: v.tensor_tensor(
                    xln[:, c * SPAN:(c + 1) * SPAN], x1ap(c, lt0, SPAN),
                    bcA[:, 0:SPAN], ALU.mult),
                    reads=[x1k(c, lt0), K("bcA"), K("xln", c)],
                    writes=[K("xln", c)], tag="l1a")
                dve(lambda v, c=c: v.tensor_tensor(
                    xln[:, c * SPAN:(c + 1) * SPAN],
                    xln[:, c * SPAN:(c + 1) * SPAN], bcB[:, 0:SPAN], ALU.subtract),
                    reads=[K("xln", c), K("bcB")], writes=[K("xln", c)], tag="l1b")

            pump(3)

            def evq(og, jj, sp=sp):
                pt, poff, pk = PS4[jj]
                dve(lambda v, og=og, pt=pt, poff=poff, sp=sp: v.tensor_copy(
                    q_sb[:, og * S + sp * SPAN: og * S + sp * SPAN + SPAN],
                    pt[0:128, poff:poff + SPAN]),
                    reads=[pk, K("q", og, sp)], writes=[K("q", og, sp)], tag="qev")
            wstrip_mm(wq_h, lambda ci, og4: [], NC, NC,
                      lambda ci: xln[:, ci * SPAN:(ci + 1) * SPAN],
                      lambda ci: [K("xln", ci)], f"q{s_i}{sp}", evq, cast=True)

            def evkv(og, jj, sp=sp):
                pt, poff, pk = PS4[jj]
                dve(lambda v, og=og, pt=pt, poff=poff, sp=sp: v.tensor_copy(
                    krec_sb[:, og * S + sp * SPAN: og * S + sp * SPAN + SPAN],
                    pt[0:128, poff:poff + SPAN]),
                    reads=[pk, K("kr", og, sp)], writes=[K("kr", og, sp)],
                    tag="kvev")
            wstrip_mm(wk_h, lambda ci, og4: [], NC, NC,
                      lambda ci: xln[:, ci * SPAN:(ci + 1) * SPAN],
                      lambda ci: [K("xln", ci)], f"kv{s_i}{sp}", evkv, cast=True,
                      extra=wv_h)

        for sp in range(NSPAN):
            for ci in range(NC):
                wb, wk = wbuf_b()
                f1, fk1 = wbuf_f()
                dma("sp", lambda e, ci=ci, f1=f1:
                    e.dma_start(f1[:, 0:DL], wdkv_h[128 * ci:128 * ci + 128, 0:DL]),
                    reads=[fk1], writes=[fk1], tag="ltwl")
                act(lambda a, f1=f1, wb=wb: a.activation(wb[:, 0:DL],
                                                         f1[:, 0:DL], AF.Copy),
                    reads=[fk1, wk], writes=[wk], tag="ltwc")
                for jj in range(NCL):
                    pt, poff, pk = PS4[jj]
                    pe(lambda t, ci=ci, jj=jj, wb=wb, pt=pt, poff=poff, sp=sp:
                       t.matmul(pt[0:128, poff:poff + SPAN],
                                wb[:, 128 * jj:128 * jj + 128],
                                krec_sb[:, ci * S + sp * SPAN:
                                        ci * S + sp * SPAN + SPAN],
                                start=(ci == 0), stop=(ci == NC - 1)),
                       reads=[wk, K("kr", ci, sp)], writes=[pk], tag="ltmm")
            for jj in range(NCL):
                pt, poff, pk = PS4[jj]
                dve(lambda v, jj=jj, pt=pt, poff=poff, sp=sp: v.tensor_copy(
                    lat_sb[:, jj * S + sp * SPAN: jj * S + sp * SPAN + SPAN],
                    pt[0:128, poff:poff + SPAN]),
                    reads=[pk, K("lat", jj, sp)], writes=[K("lat", jj, sp)],
                    tag="ltev")

        for sp in range(NSPAN):
            def evkr(og, jj, sp=sp):
                pt, poff, pk = PS4[jj]
                dve(lambda v, og=og, pt=pt, poff=poff, sp=sp: v.tensor_copy(
                    krec_sb[:, og * S + sp * SPAN: og * S + sp * SPAN + SPAN],
                    pt[0:128, poff:poff + SPAN]),
                    reads=[pk, K("kr", og, sp)], writes=[K("kr", og, sp)],
                    tag="krev")
            wstrip_mm(wupk_h, lambda ci, og4: [], NCL, NC,
                      lambda ci, sp=sp: lat_sb[:, ci * S + sp * SPAN:
                                               ci * S + sp * SPAN + SPAN],
                      lambda ci, sp=sp: [K("lat", ci, sp)], f"kr{s_i}{sp}", evkr,
                      cast=True)

        for sp in range(NSPAN):
            def evvr(og, jj, sp=sp):
                pt, poff, pk = PS4[jj]
                dve(lambda v, pt=pt, poff=poff: v.tensor_copy(
                    xstgb[:, 0:SPAN], pt[0:128, poff:poff + SPAN]),
                    reads=[pk, K("xstgb")], writes=[K("xstgb")], tag="vev")
                for b in range(SPAN // 128):
                    kc = sp * (SPAN // 128) + b
                    pe(lambda t, b=b: t.transpose(
                        psB[:, 0:128], xstgb[:, 128 * b:128 * b + 128], cId[:, :]),
                       reads=[K("xstgb"), K("cId")], writes=[K("psB")], tag="vtr")
                    for hh in range(2):
                        hidx = 2 * og + hh
                        dve(lambda v, kc=kc, hidx=hidx, hh=hh: v.tensor_copy(
                            v65[:, kc * V65W + hidx * 65:
                                 kc * V65W + hidx * 65 + 64],
                            psB[:, 64 * hh:64 * hh + 64]),
                            reads=[K("psB"), K("v65", kc)], writes=[K("v65", kc)],
                            tag="v65e")
            wstrip_mm(wupv_h, lambda ci, og4: [], NCL, NC,
                      lambda ci, sp=sp: lat_sb[:, ci * S + sp * SPAN:
                                               ci * S + sp * SPAN + SPAN],
                      lambda ci, sp=sp: [K("lat", ci, sp)], f"vr{s_i}{sp}", evvr,
                      cast=True)
        for kc in range(NKC):
            def emones(v, kc=kc):
                apv = bass.AP(v65, kc * V65W + 64,
                              [[NKC * V65W, 128], [65, H], [1, 1]])
                return v.memset(apv, 1.0)
            dve(emones, reads=[K("v65", kc)], writes=[K("v65", kc)], tag="v65o")

        phase_ctx.close()
        phase_ctx2 = ExitStack()
        qsl = sb("qsl%d" % s_i, [128, 2 * SPAN], BF16, phase_ctx2)
        pp = sb("pp%d" % s_i, [128, 2048], BF16, phase_ctx2)
        pj = sb("pj%d" % s_i, [128, NC * SPAN], BF16, phase_ctx2)
        sch.fence("dve", sch.keys_with_names(LATK),
                  [K("qsl"), K("pp", 0), K("pp", 1)] +
                  [K("pj", c) for c in range(NC)], tag="fence_score")

        def scores_span(sp):
            aob = ao0 if sp % 2 == 0 else ao1
            aokey = K("ao", sp % 2)
            q0 = sp * SPAN
            for pair in range(NPAIR):
                pump(2)
                dve(lambda v, pair=pair, q0=q0: v.tensor_tensor(
                    qsl[:, 0:SPAN], q_sb[:, pair * S + q0: pair * S + q0 + SPAN],
                    cD[:, q0:q0 + SPAN], ALU.mult),
                    reads=[K("q", pair, sp), K("cD"), K("qsl")], writes=[K("qsl")],
                    tag="qsD")
                dve(lambda v, pair=pair, q0=q0: v.tensor_tensor(
                    qsl[:, SPAN:2 * SPAN],
                    q_sb[:, pair * S + q0: pair * S + q0 + SPAN],
                    cE[:, q0:q0 + SPAN], ALU.mult),
                    reads=[K("q", pair, sp), K("cE"), K("qsl")], writes=[K("qsl")],
                    tag="qsE")
                def emit_qk(kc, pair=pair, q0=q0):
                    pbuf = psP1 if kc % 2 == 0 else psP2
                    pks = PK_P1 if kc % 2 == 0 else PK_P2
                    kc0 = kc * 128
                    kkey = K("kr", pair, (kc * 128) // SPAN)
                    hi_all = kc0 > q0 + SPAN - 1
                    qoff = SPAN if hi_all else 0
                    for hh in range(2):
                        lo = 64 * hh
                        po = 512 * hh
                        pe(lambda t, pair=pair, kc=kc, pbuf=pbuf, po=po,
                           qoff=qoff, lo=lo:
                           t.matmul(pbuf[0:128, po:po + SPAN],
                                    krec_sb[lo:lo + 64,
                                            pair * S + kc * 128:
                                            pair * S + kc * 128 + 128],
                                    qsl[lo:lo + 64, qoff:qoff + SPAN],
                                    start=True, stop=True),
                           reads=[kkey, K("qsl")], writes=pks, tag="qk")

                emit_qk(0)
                for kc in range(NKC):
                    pbuf = psP1 if kc % 2 == 0 else psP2
                    pks = PK_P1 if kc % 2 == 0 else PK_P2
                    ppo = (kc % 2) * 1024
                    ppk = K("pp", kc % 2)
                    kc0 = kc * 128
                    hi_all = kc0 > q0 + SPAN - 1
                    mixed = (not hi_all) and (kc0 + 127 >= q0)
                    if mixed:
                        qbd = (kc0 - q0) // 128
                        for hh in range(2):
                            po = 512 * hh
                            if qbd > 0:
                                dve(lambda v, pbuf=pbuf, po=po, qbd=qbd, q0=q0, kc=kc:
                                    v.scalar_tensor_tensor(
                                        pbuf[:, po:po + 128 * qbd],
                                        pbuf[:, po:po + 128 * qbd],
                                        ecol2[:, kc:kc + 1],
                                        cE2[:, q0:q0 + 128 * qbd],
                                        ALU.mult, ALU.mult),
                                    reads=pks + [K("ecol2"), K("cE2")], writes=pks,
                                    tag="ufix")
                            dve(lambda v, pbuf=pbuf, po=po, qbd=qbd: v.tensor_tensor(
                                pbuf[:, po + 128 * qbd:po + 128 * qbd + 128],
                                pbuf[:, po + 128 * qbd:po + 128 * qbd + 128],
                                cC[:, :], ALU.mult),
                                reads=pks + [K("cC")], writes=pks, tag="cfx")
                    ecol = ecolD if hi_all else ecolE
                    eck = K("ecolD") if hi_all else K("ecolE")
                    if SPAN == 512:
                        act(lambda s, pbuf=pbuf, ppo=ppo, kc=kc, ecol=ecol:
                            s.activation(pp[:, ppo:ppo + 1024], pbuf[:, :],
                                         AF.Exp, scale=ecol[:, kc:kc + 1]),
                            reads=pks + [eck, ppk], writes=[ppk], tag="ex")
                    else:
                        for hh in range(2):
                            po = 512 * hh
                            act(lambda s, pbuf=pbuf, ppo=ppo, po=po, kc=kc,
                                ecol=ecol:
                                s.activation(pp[:, ppo + po:ppo + po + SPAN],
                                             pbuf[:, po:po + SPAN], AF.Exp,
                                             scale=ecol[:, kc:kc + 1]),
                                reads=pks + [eck, ppk], writes=[ppk], tag="ex2")
                    if kc + 1 < NKC:
                        emit_qk(kc + 1)
                    for hh in range(2):
                        hidx = 2 * pair + hh
                        pq = psQ1 if hh == 0 else psQ2
                        pqk = K("psQ1") if hh == 0 else K("psQ2")
                        pe(lambda t, kc=kc, hidx=hidx, pq=pq, ppo=ppo, hh=hh:
                           t.matmul(pq[0:65, 0:SPAN],
                                    v65[:, kc * V65W + hidx * 65:
                                         kc * V65W + hidx * 65 + 65],
                                    pp[:, ppo + 512 * hh:ppo + 512 * hh + SPAN],
                                    start=(kc == 0), stop=(kc == NKC - 1)),
                           reads=[K("v65", kc), ppk], writes=[pqk], tag="pv")
                for hh in range(2):
                    pq = psQ1 if hh == 0 else psQ2
                    pqk = K("psQ1") if hh == 0 else K("psQ2")
                    zt = zst if hh == 0 else zst2
                    ztk = K("zst", hh)
                    dve(lambda v, pair=pair, hh=hh, pq=pq, aob=aob: v.tensor_copy(
                        aob[64 * hh:64 * hh + 64, pair * SPAN:(pair + 1) * SPAN],
                        pq[0:64, 0:SPAN]),
                        reads=[pqk, aokey], writes=[aokey], tag="aoe")
                    dve(lambda v, pq=pq, zt=zt: v.tensor_copy(
                        zt[64:65, 0:SPAN], pq[64:65, 0:SPAN]),
                        reads=[pqk, ztk], writes=[ztk], tag="zev")
                    zw0 = (H * SPAN) // 128
                    dma("sp", lambda e, pair=pair, hh=hh, zt=zt, zw0=zw0:
                        e.dma_start(
                            zinb[8 * (2 * pair + hh):8 * (2 * pair + hh) + 8,
                                 0:zw0],
                            zt[64:65, 0:SPAN]),
                        reads=[ztk], writes=[K("zd", pair, hh)], tag="zs")
            zw = (H * SPAN) // 128

            def _zr(v, zw=zw):
                with nc.allow_low_precision(reason="1/Z stored bf16 as before"):
                    return v.reciprocal(zoutf[:, 0:zw], zinb[:, 0:zw])
            dve(_zr,
                reads=[K("zd", p2, h2) for p2 in range(NPAIR)
                       for h2 in range(2)] + [K("zinb"), K("zoutf")],
                writes=[K("zoutf")], tag="zr")
            for pair in range(NPAIR):
                dma("sp", lambda e, pair=pair, zw=zw: e.dma_start(
                    zprb[:, 0:SPAN], zoutf[16 * pair:16 * pair + 16, 0:zw]),
                    reads=[K("zoutf"), K("zprb")], writes=[K("zprb")], tag="zp")
                pe(lambda t: t.matmul(psM[:, 0:SPAN], pairsel[:, :],
                                      zprb[:, 0:SPAN], start=True, stop=True),
                   reads=[K("pairsel"), K("zprb")], writes=[K("psM")], tag="zb")
                dve(lambda v: v.tensor_copy(xstgb[:, 0:SPAN], psM[:, 0:SPAN]),
                    reads=[K("psM"), K("xstgb")], writes=[K("xstgb")], tag="zbe")
                dve(lambda v, pair=pair, aob=aob: v.tensor_tensor(
                    aob[:, pair * SPAN:(pair + 1) * SPAN],
                    aob[:, pair * SPAN:(pair + 1) * SPAN], xstgb[:, 0:SPAN],
                    ALU.mult),
                    reads=[aokey, K("xstgb")], writes=[aokey], tag="zap")

        def finish_span(sp):
            aob = ao0 if sp % 2 == 0 else ao1
            aokey = K("ao", sp % 2)
            lt0 = sp * SPAN

            def evo(og, jj):
                pt, poff, pk = PS4[jj]
                dve(lambda v, og=og, pt=pt, poff=poff: v.tensor_copy(
                    pj[:, og * SPAN:(og + 1) * SPAN], pt[0:128, poff:poff + SPAN]),
                    reads=[pk, K("pj", og)], writes=[K("pj", og)], tag="woe")
            wstrip_mm(wo_h, lambda ci, og4: [], NC, NC,
                      lambda ci: aob[:, ci * SPAN:(ci + 1) * SPAN],
                      lambda ci: [aokey], f"wo{s_i}{sp}", evo, cast=True)
            ln_stats(lambda c: pj[:, c * SPAN:(c + 1) * SPAN],
                     lambda c: [K("pj", c)], f"l2a_{s_i}_{sp}_")
            for c in range(NC):
                dve(lambda v, c=c: v.tensor_tensor(
                    pj[:, c * SPAN:(c + 1) * SPAN], pj[:, c * SPAN:(c + 1) * SPAN],
                    bcA[:, 0:SPAN], ALU.mult),
                    reads=[K("pj", c), K("bcA")], writes=[K("pj", c)], tag="l2m")
                dve(lambda v, c=c: v.tensor_tensor(
                    pj[:, c * SPAN:(c + 1) * SPAN], pj[:, c * SPAN:(c + 1) * SPAN],
                    bcB[:, 0:SPAN], ALU.subtract),
                    reads=[K("pj", c), K("bcB")], writes=[K("pj", c)], tag="l2s")
                dve(lambda v, c=c, lt0=lt0: v.tensor_tensor(
                    x1ap(c, lt0, SPAN), x1ap(c, lt0, SPAN),
                    pj[:, c * SPAN:(c + 1) * SPAN], ALU.add),
                    reads=[K("pj", c), x1k(c, lt0)], writes=[x1k(c, lt0)],
                    tag="res1")

        for sp in range(NSPAN):
            scores_span(sp)
            pump(4)
            if sp >= 1:
                finish_span(sp - 1)
        finish_span(NSPAN - 1)

        phase_ctx2.close()
        for c in range(NC):
            for sp in range(NSPAN):
                dma("sp", lambda e, c=c, sp=sp, s_i=s_i: e.dma_start(
                    x1d_h[:, s_i * NC * S + c * S + sp * SPAN:
                          s_i * NC * S + c * S + sp * SPAN + SPAN],
                    x1ap(c, sp * SPAN, SPAN)),
                    reads=[x1k(c, sp * SPAN)], writes=[K("x1d", s_i, c, sp)],
                    tag="x1sp")

    for s_i in range(NSEQ):
        seq_run(s_i)
    seq_ctx.close()

    PPF.extend([(wfa, K("wfa")), (wfb, K("wfb"))])
    PPB.extend([(wba, K("wba")), (wbb, K("wbb"))])
    PPS[:] = [(wbc, K("wbc")), (wbd, K("wbd")), (wqB, K("wqB"))]
    pump(10 ** 9)
    ffn_ctx = ExitStack()
    sch.fence("dve", sch.keys_with_names(
        ["x1", "q", "kr", "lat", "v65", "ao", "pj", "zst", "qsl", "pp", "xln",
         "wta", "wtb", "wpa", "wpb", "wqA", "wqB", "wfa", "wfb", "wba", "wbb",
         "wbc", "wbd", "psB2"]),
        ([K("x1l", p, c) for p in range(2) for c in range(NC)] +
         [K("aq", c) for c in range(NC)] +
         [K("h", og) for og in range(NCI)] + [K("y", og) for og in range(NC)] +
         [K("gscr", j) for j in range(4)]), tag="fence_ffn")
    x1ld = sb("x1ld", [128, NC * SPAN], BF16, ffn_ctx)
    x1ldB = sb("x1ldB", [128, NC * SPAN], BF16, ffn_ctx)
    aq = sb("aq", [128, NC * SPAN], BF16, ffn_ctx)
    hsp = sb("hsp", [128, NCI * SPAN], BF16, ffn_ctx)
    ysp = sb("ysp", [128, NC * SPAN], BF16, ffn_ctx)
    gscr = sb("gscr", [128, 4 * 512], BF16, ffn_ctx)
    rA2 = sb("rA2", [1, 512], F32, ffn_ctx)
    rB2 = sb("rB2", [1, 512], F32, ffn_ctx)
    mr2 = sb("mr2", [1, 512], F32, ffn_ctx)
    rbA2 = sb("rbA2", [1, 512], BF16, ffn_ctx)
    rbB2 = sb("rbB2", [1, 512], BF16, ffn_ctx)
    bcA2 = sb("bcA2", [128, 512], BF16, ffn_ctx)
    bcB2 = sb("bcB2", [128, 512], BF16, ffn_ctx)
    xst2 = sb("xst2", [128, 512], BF16, ffn_ctx)
    syb = sb("syb", [128, 512], BF16, ffn_ctx)
    KIT1 = Kit(psS=psQ2, psSk=K("psQ2"), psQ=psQ2, psQk=K("psQ2"), single=True,
               r1=rA2, r1k=K("rA2"), r2=rB2, r2k=K("rB2"),
               rb1=rbA2, rb1k=K("rbA2"), rb2=rbB2, rb2k=K("rbB2"),
               bcA=bcA2, bcAk=K("bcA2"), bcB=bcB2, bcBk=K("bcB2"),
               xs=xst2, xsk=K("xst2"), m2=mr2, m2k=K("mr2"))

    def quant_scales(src, nchunks, src_keys, need_sr, uid, sr_cols=None,
                     sr_dst=None):
        view = src.rearrange("p (c t) -> p t c", c=nchunks)
        dve(lambda v: v.tensor_reduce(xstgb[:, 0:SPAN], view, AX.X, ALU.max,
                                      apply_absolute_value=True),
            reads=src_keys + [K("xstgb")], writes=[K("xstgb")], tag=f"{uid}mx")
        nb = SPAN // 128
        for b in range(nb):
            pe(lambda t, b=b: t.transpose(
                psB[:, 128 * b:128 * b + 128], xstgb[:, 128 * b:128 * b + 128],
                cId[:, :]),
               reads=[K("xstgb"), K("cId")], writes=[K("psB")], tag=f"{uid}tr")
        for b in range(nb):
            dve(lambda v, b=b: v.tensor_reduce(
                scc[:, 3 + b:4 + b], psB[:, 128 * b:128 * b + 128], AX.X, ALU.max),
                reads=[K("psB"), K("smax")], writes=[K("smax")], tag=f"{uid}rm")
        dve(lambda v, nb=nb: v.tensor_scalar(
            scc[:, 3:3 + nb], scc[:, 3:3 + nb], 1e-5, None, ALU.max),
            reads=[K("smax")], writes=[K("smax")], tag=f"{uid}cl")
        dve(lambda v, nb=nb: v.reciprocal(scc[:, 8:8 + nb], scc[:, 3:3 + nb]),
            reads=[K("smax"), K("srec")], writes=[K("srec")], tag=f"{uid}rc")
        dve(lambda v, nb=nb: v.tensor_scalar(
            scc[:, 8:8 + nb], scc[:, 8:8 + nb], 127.0, None, ALU.mult),
            reads=[K("srec")], writes=[K("srec")], tag=f"{uid}s")

        for b in range(nb):
            dma("sp", lambda e, b=b: e.dma_start(
                r1[0:1, 128 * b:128 * b + 128], scc[:, 8 + b:9 + b]),
                reads=[K("srec"), K("r1")], writes=[K("r1")], tag=f"{uid}br")
        pe(lambda t: t.matmul(psM[:, 0:SPAN], onesrow_f[:, :], r1[0:1, 0:SPAN],
                              start=True, stop=True),
           reads=[K("onesrow_f"), K("r1")], writes=[K("psM")], tag=f"{uid}bm")
        act(lambda a: a.activation(bcA[:, 0:SPAN], psM[:, 0:SPAN], AF.Copy),
            reads=[K("psM"), K("bcA")], writes=[K("bcA")], tag=f"{uid}be")
        if need_sr:
            if sr_cols is None:
                dve(lambda v, nb=nb: v.tensor_scalar(
                    scc[:, 12:12 + nb], scc[:, 3:3 + nb], 1.0 / 127.0, None,
                    ALU.mult),
                    reads=[K("smax"), K("sr")], writes=[K("sr")], tag=f"{uid}sr")
            else:
                c1, c2, imm = sr_cols
                dve(lambda v, nb=nb, c1=c1, c2=c2: v.tensor_scalar(
                    scc[:, 12:12 + nb], scc[:, 3:3 + nb], c1, c2, ALU.mult,
                    ALU.mult),
                    reads=[K("smax"), K("sr"), K("thr")], writes=[K("sr")],
                    tag=f"{uid}sra")
                dve(lambda v, nb=nb, imm=imm: v.tensor_scalar(
                    scc[:, 12:12 + nb], scc[:, 12:12 + nb], imm, None, ALU.mult),
                    reads=[K("sr")], writes=[K("sr")], tag=f"{uid}srb")

            for b in range(nb):
                dma("sp", lambda e, b=b: e.dma_start(
                    r2[0:1, 128 * b:128 * b + 128], scc[:, 12 + b:13 + b]),
                    reads=[K("sr"), K("r2")], writes=[K("r2")], tag=f"{uid}b2")
            sdst, sdk = (bcB, K("bcB")) if sr_dst is None else sr_dst
            pe(lambda t: t.matmul(psM[:, 0:SPAN], onesrow_f[:, :],
                                  r2[0:1, 0:SPAN], start=True, stop=True),
               reads=[K("onesrow_f"), K("r2")], writes=[K("psM")], tag=f"{uid}m2")
            act(lambda a, sdst=sdst: a.activation(sdst[:, 0:SPAN],
                                                  psM[:, 0:SPAN], AF.Copy),
                reads=[K("psM"), sdk], writes=[sdk], tag=f"{uid}e2")

    def round_gadget(cap, ckey, sr_tile, dbg=None):
        dve(lambda v: v.tensor_tensor(gA[:, 0:SPAN], cap, bcA[:, 0:SPAN],
                                      ALU.mult),
            reads=[ckey, K("bcA"), K("wta")], writes=[K("wta")], tag="rg1")
        dve(lambda v: v.tensor_scalar(scrI[:, 0:SPAN], gA[:, 0:SPAN], KOFF,
                                      None, ALU.add),
            reads=[K("wta"), K("scrI")], writes=[K("scrI")], tag="rg2")
        dve(lambda v: v.tensor_copy(xstgb[:, 0:SPAN], scrI[:, 0:SPAN]),
            reads=[K("scrI"), K("xstgb")], writes=[K("xstgb")], tag="rg2b")
        dve(lambda v: v.scalar_tensor_tensor(
            scrA[:, 0:SPAN], gA[:, 0:SPAN], KOFF, xstgb[:, 0:SPAN], ALU.add,
            ALU.subtract),
            reads=[K("wta"), K("xstgb"), K("scrA")], writes=[K("scrA")], tag="rg3")
        if dbg is not None:
            d1, d2, d3 = dbg
            dma("pool", lambda e: e.dma_start(d1[:, 0:SPAN], scrA[:, 0:SPAN]),
                reads=[K("scrA")], writes=[K("dbgD")], tag="dbgD")
            dma("pool", lambda e: e.dma_start(d2[:, 0:SPAN], xstgb[:, 0:SPAN]),
                reads=[K("xstgb")], writes=[K("dbgE")], tag="dbgE")
            dma("pool", lambda e: e.dma_start(d3[:, 0:SPAN], wtb[:, 0:SPAN]),
                reads=[K("wtb")], writes=[K("dbgF")], tag="dbgF")
        dve(lambda v: v.tensor_scalar(gB[:, 0:SPAN], scrA[:, 0:SPAN], 0.5,
                                      None, ALU.is_ge),
            reads=[K("scrA"), K("wtb")], writes=[K("wtb")], tag="rg4")
        dve(lambda v: v.scalar_tensor_tensor(
            gB[:, 0:SPAN], scrA[:, 0:SPAN], -0.5, gB[:, 0:SPAN],
            ALU.is_gt, ALU.add),
            reads=[K("scrA"), K("wtb")], writes=[K("wtb")], tag="rg5")
        if sr_tile is not None:
            dve(lambda v: v.scalar_tensor_tensor(
                gB[:, 0:SPAN], gB[:, 0:SPAN], -(1.0 + KOFF), xstgb[:, 0:SPAN],
                ALU.add, ALU.add),
                reads=[K("wtb"), K("xstgb")], writes=[K("wtb")],
                tag="rg6")
            dve(lambda v: v.tensor_tensor(cap, gB[:, 0:SPAN], sr_tile,
                                          ALU.mult),
                reads=[K("wtb"), K("bcB"), ckey], writes=[ckey], tag="rg7")
        else:
            dve(lambda v: v.scalar_tensor_tensor(
                cap, gB[:, 0:SPAN], -1.0, xstgb[:, 0:SPAN], ALU.add, ALU.add),
                reads=[K("wtb"), K("xstgb"), ckey], writes=[ckey], tag="rg6b")

    def x1buf(fs):
        return (x1ld, "x1ld") if fs % 2 == 0 else (x1ldB, "x1ldB")

    def ffn_x1load(fs):
        ts0 = fs * SPAN
        s_i = ts0 // S
        lt0 = ts0 % S
        xb, xbn = x1buf(fs)
        par = fs % 2
        for c in range(NC):
            dma("sp", lambda e, c=c, s_i=s_i, lt0=lt0, xb=xb: e.dma_start(
                xb[:, c * SPAN:(c + 1) * SPAN],
                x1d_h[:, s_i * NC * S + c * S + lt0:
                      s_i * NC * S + c * S + lt0 + SPAN]),
                reads=[K("x1d", s_i, c, lt0 // SPAN), K("x1l", par, c)],
                writes=[K("x1l", par, c)], tag="x1ld")

    def ffn_head_stats(fs):
        xb, xbn = x1buf(fs)
        par = fs % 2
        ln_stats_a(lambda c, xb=xb: xb[:, c * SPAN:(c + 1) * SPAN],
                   lambda c, par=par: [K("x1l", par, c)], f"l2_{fs}_")

    def ffn_head(fs):
        ts0 = fs * SPAN
        xb, xbn = x1buf(fs)
        par = fs % 2
        ln_stats_b(f"l2_{fs}_")
        for c in range(NC):
            dve(lambda v, c=c, xb=xb: v.tensor_tensor(
                aq[:, c * SPAN:(c + 1) * SPAN],
                xb[:, c * SPAN:(c + 1) * SPAN], bcA[:, 0:SPAN], ALU.mult),
                reads=[K("x1l", par, c), K("bcA"), K("aq", c)],
                writes=[K("aq", c)], tag="l2a")
            dve(lambda v, c=c: v.tensor_tensor(
                aq[:, c * SPAN:(c + 1) * SPAN], aq[:, c * SPAN:(c + 1) * SPAN],
                bcB[:, 0:SPAN], ALU.subtract),
                reads=[K("aq", c), K("bcB")], writes=[K("aq", c)], tag="l2b")
        quant_scales(aq[:, 0:NC * SPAN], NC, [K("aq", c) for c in range(NC)],
                     True, f"qx{fs}")
        if DBG and fs == 0:
            dma("sp", lambda e: e.dma_start(dbgA_h[:, 0:SPAN], bcA[:, 0:SPAN]),
                reads=[K("bcA")], writes=[K("dbgA")], tag="dbgA")
            dma("sp", lambda e: e.dma_start(dbgB_h[:, 0:SPAN], bcB[:, 0:SPAN]),
                reads=[K("bcB")], writes=[K("dbgB")], tag="dbgB")
        for c in range(NC):
            round_gadget(aq[:, c * SPAN:(c + 1) * SPAN], K("aq", c),
                         bcB[:, 0:SPAN],
                         dbg=(dbgD_h, dbgE_h, dbgF_h)
                         if (DBG and fs == 0 and c == 0) else None)
        if DBG and fs == 0:
            dma("sp", lambda e: e.dma_start(dbgC_h[:, 0:SPAN],
                                            aq[:, 0:SPAN]),
                reads=[K("aq", 0)], writes=[K("dbgC")], tag="dbgC")

    def ffn_gateup(fs):
        for og4 in range(NCI // 4):
            for ci in range(NC):
                wb, wk = wbuf_b()
                dma("sp", lambda e, og4=og4, ci=ci, wb=wb:
                    e.dma_start(wb[:, :], wgt_h[128 * ci:128 * ci + 128,
                                                512 * og4:512 * og4 + 512]),
                    reads=[wk] + wq_strip_keys("wg", ci, og4), writes=[wk],
                    tag="gwl")
                for jj in range(4):
                    pt, poff, pk = PS4[jj]
                    pe(lambda t, ci=ci, jj=jj, wb=wb, pt=pt, poff=poff:
                       t.matmul(pt[0:128, poff:poff + SPAN],
                                wb[:, 128 * jj:128 * jj + 128],
                                aq[:, ci * SPAN:(ci + 1) * SPAN],
                                start=(ci == 0), stop=(ci == NC - 1)),
                       reads=[wk, K("aq", ci)], writes=[pk], tag="gmm")
            for jj in range(4):
                pt, poff, pk = PS4[jj]
                act(lambda s, jj=jj, pt=pt, poff=poff: s.activation(
                    gscr[:, jj * 512:jj * 512 + SPAN],
                    pt[0:128, poff:poff + SPAN], AF.Silu, scale=mgcol[:, 0:1]),
                    reads=[pk, K("mgcol"), K("gscr", jj)],
                    writes=[K("gscr", jj)], tag="sil")
            for ci in range(NC):
                wb, wk = wbuf_b()
                dma("sp", lambda e, og4=og4, ci=ci, wb=wb:
                    e.dma_start(wb[:, :], wut_h[128 * ci:128 * ci + 128,
                                                512 * og4:512 * og4 + 512]),
                    reads=[wk] + wq_strip_keys("wu", ci, og4), writes=[wk],
                    tag="uwl")
                for jj in range(4):
                    pt, poff, pk = PS4[jj]
                    pe(lambda t, ci=ci, jj=jj, wb=wb, pt=pt, poff=poff:
                       t.matmul(pt[0:128, poff:poff + SPAN],
                                wb[:, 128 * jj:128 * jj + 128],
                                aq[:, ci * SPAN:(ci + 1) * SPAN],
                                start=(ci == 0), stop=(ci == NC - 1)),
                       reads=[wk, K("aq", ci)], writes=[pk], tag="umm")
            for jj in range(4):
                og = og4 * 4 + jj
                pt, poff, pk = PS4[jj]
                dve(lambda v, og=og, jj=jj, pt=pt, poff=poff: v.tensor_tensor(
                    hsp[:, og * SPAN:(og + 1) * SPAN],
                    pt[0:128, poff:poff + SPAN],
                    gscr[:, jj * 512:jj * 512 + SPAN], ALU.mult),
                    reads=[pk, K("gscr", jj), K("h", og)], writes=[K("h", og)],
                    tag="hm")
    def ffn_hquant(fs):
        quant_scales(hsp[:, 0:NCI * SPAN], NCI,
                     [K("h", og) for og in range(NCI)], True, f"qh{fs}",
                     sr_cols=(thrcol[:, 2:3], thrcol[:, 4:5], 4.0 / 127.0),
                     sr_dst=(syb, K("syb")))
        for c in range(NCI):
            round_gadget(hsp[:, c * SPAN:(c + 1) * SPAN], K("h", c), None)

    def ffn_down(fs):
        for og4 in range(NC // 4):
            for ci in range(NCI):
                wb, wk = wbuf_b()
                dma("sp", lambda e, og4=og4, ci=ci, wb=wb:
                    e.dma_start(wb[:, :], wdt_h[128 * ci:128 * ci + 128,
                                                512 * og4:512 * og4 + 512]),
                    reads=[wk] + wq_strip_keys("wd", ci, og4), writes=[wk],
                    tag="dwl")
                for jj in range(4):
                    pt, poff, pk = PS4[jj]
                    pe(lambda t, ci=ci, jj=jj, wb=wb, pt=pt, poff=poff:
                       t.matmul(pt[0:128, poff:poff + SPAN],
                                wb[:, 128 * jj:128 * jj + 128],
                                hsp[:, ci * SPAN:(ci + 1) * SPAN],
                                start=(ci == 0), stop=(ci == NCI - 1)),
                       reads=[wk, K("h", ci)], writes=[pk], tag="dmm")
            for jj in range(4):
                og = og4 * 4 + jj
                pt, poff, pk = PS4[jj]
                dve(lambda v, og=og, pt=pt, poff=poff: v.scalar_tensor_tensor(
                    ysp[:, og * SPAN:(og + 1) * SPAN],
                    pt[0:128, poff:poff + SPAN], colsumK[:, og:og + 1],
                    syb[:, 0:SPAN], ALU.subtract, ALU.mult),
                    reads=[pk, K("csK"), K("syb"), K("y", og)],
                    writes=[K("y", og)], tag="dev")

    def ffn_tail(fs):
        ts0 = fs * SPAN
        xb, xbn = x1buf(fs)
        par = fs % 2
        ln_stats(lambda c: ysp[:, c * SPAN:(c + 1) * SPAN],
                 lambda c: [K("y", c)], f"l2b_{fs}_", kt=KIT1)
        for c in range(NC):
            dve(lambda v, c=c: v.tensor_tensor(
                ysp[:, c * SPAN:(c + 1) * SPAN],
                ysp[:, c * SPAN:(c + 1) * SPAN], bcA2[:, 0:SPAN], ALU.mult),
                reads=[K("y", c), K("bcA2")], writes=[K("y", c)], tag="ba")
            dve(lambda v, c=c: v.tensor_tensor(
                ysp[:, c * SPAN:(c + 1) * SPAN],
                ysp[:, c * SPAN:(c + 1) * SPAN], bcB2[:, 0:SPAN], ALU.subtract),
                reads=[K("y", c), K("bcB2")], writes=[K("y", c)], tag="bb")
            dve(lambda v, c=c, xb=xb: v.tensor_tensor(
                ysp[:, c * SPAN:(c + 1) * SPAN],
                ysp[:, c * SPAN:(c + 1) * SPAN],
                xb[:, c * SPAN:(c + 1) * SPAN], ALU.add),
                reads=[K("y", c), K("x1l", par, c)], writes=[K("y", c)],
                tag="rs2")
        for b in range(SPAN // 128):
            tg = ts0 + b * 128
            for g in range(2):
                for bb in range(4):
                    c = 4 * g + bb
                    pe(lambda t, c=c, b=b, bb=bb: t.transpose(
                        psB[:, 512 + 128 * bb:512 + 128 * bb + 128],
                        ysp[:, c * SPAN + 128 * b: c * SPAN + 128 * b + 128],
                        cId[:, :]),
                       reads=[K("y", c), K("cId")], writes=[K("psB2")],
                       tag="otr")
                act(lambda a: a.activation(xstage[:, 0:512], psB[:, 512:1024],
                                           AF.Copy),
                    reads=[K("psB2"), K("xstage")], writes=[K("xstage")],
                    tag="oev")
                dma("sp", lambda e, tg=tg, g=g: e.dma_start(
                    out_h[tg:tg + 128, 512 * g:512 * g + 512], xstage[:, 0:512]),
                    reads=[K("xstage")], writes=[K("outd", tg, g)], tag="ost")

    def ffn_run():
        ffn_x1load(0)
        ffn_head_stats(0)
        ffn_head(0)
        ffn_gateup(0)
        for fs in range(NFS):
            if fs + 1 < NFS:
                ffn_x1load(fs + 1)
            ffn_hquant(fs)
            if fs + 1 < NFS:
                ffn_head_stats(fs + 1)
            ffn_down(fs)
            if fs + 1 < NFS:
                ffn_head(fs + 1)
                ffn_gateup(fs + 1)
            ffn_tail(fs)

    ffn_run()
    ffn_ctx.close()

    tap_src = {"cD": cD, "cC": cC, "colsumK": colsumK, "thrcol": thrcol}
    for tname, pdim, fdim, tdt in taps:
        src = tap_src.get(tname)
        if src is None:
            continue
        dma("sp", lambda e, tname=tname, src=src, pdim=pdim, fdim=fdim:
            e.dma_start(tap_h[tname][:, :], src[0:pdim, 0:fdim]),
            reads=[], writes=[K("tap", tname)], tag="tap")

    outkeys = [K("outd", tg, g) for tg in range(0, T, 128) for g in range(2)]
    sch.op("sp", lambda e: e.nop(), reads=outkeys, writes=[], tag="fin")

    with ctx:
        with ExitStack() as sctx:
            for sname in sch.sem_names:
                sems[sname] = sctx.enter_context(nc.semaphore("sem_" + sname))
            with nc.Block() as block:
                sch.emit(block, sems)
    return nc, sch


DEFAULT_CFG = {"NSEQ": 2, "S": 2048, "D": 1024, "DL": 256, "INNER": 4096, "H": 16}


def kernel(**inputs):
    cfg = DEFAULT_CFG
    x = np.ascontiguousarray(np.asarray(inputs["L0_attn_x"], dtype=np.float32))
    B, S, D = x.shape
    ncores = 8
    seq_per = B // ncores
    wmap = {
        "W_q": np.ascontiguousarray(np.asarray(inputs["L0_attn_W_q"], np.float32)),
        "W_k": np.ascontiguousarray(np.asarray(inputs["L0_attn_W_k"], np.float32)),
        "W_v": np.ascontiguousarray(np.asarray(inputs["L0_attn_W_v"], np.float32)),
        "W_o": np.ascontiguousarray(np.asarray(inputs["L0_attn_W_o"], np.float32)),
        "W_down_kv": np.ascontiguousarray(
            np.asarray(inputs["L0_attn_W_down_kv"], np.float32)),
        "W_up_k": np.ascontiguousarray(np.asarray(inputs["L0_attn_W_up_k"], np.float32)),
        "W_up_v": np.ascontiguousarray(np.asarray(inputs["L0_attn_W_up_v"], np.float32)),
        "w_gate": np.ascontiguousarray(np.asarray(inputs["L0_ffn_w_gate"], np.float32)),
        "w_up": np.ascontiguousarray(np.asarray(inputs["L0_ffn_w_up"], np.float32)),
        "w_down": np.ascontiguousarray(np.asarray(inputs["L0_ffn_w_down"], np.float32)),
    }
    nc, _ = build_program(cfg)
    in_maps = []
    for c in range(ncores):
        m = dict(wmap)
        m["x"] = np.ascontiguousarray(
            x[c * seq_per:(c + 1) * seq_per].reshape(seq_per * S, D))
        in_maps.append(m)
    res = run_bass_kernel_spmd(nc, in_maps, core_ids=list(range(ncores)))
    out = np.concatenate(
        [r["out"].reshape(seq_per, S, D) for r in res.results], axis=0)
    return np.ascontiguousarray(out.astype(np.float32))


# version 14
# speedup vs baseline: 1.0224x; 1.0224x over previous
import numpy as np
from collections import deque
from contextlib import ExitStack

import concourse.bass as bass
import concourse.mybir as mybir
from concourse.bass_utils import run_bass_kernel_spmd

F32 = mybir.dt.float32
BF16 = mybir.dt.bfloat16
I16 = mybir.dt.int16
AF = mybir.ActivationFunctionType
ALU = mybir.AluOpType
AX = mybir.AxisListType

LAM = 0.01
EPS = 1e-5
KOFF = 128.0

ENGINES = ("pe", "act", "dve", "sp", "pool")


class Sched:

    NLANE = 12

    def __init__(self):
        self.sem_names = ["pe", "act", "dve", "sp", "pool"]
        for q in ("sp", "pool"):
            for l in range(self.NLANE):
                self.sem_names.append(f"{q}_l{l}")
        self.cnt = {sname: 0 for sname in self.sem_names}
        self.ops = {e: [] for e in ENGINES}
        self.lane_i = {"sp": 0, "pool": 0}
        self.writer = {}
        self.readers = {}
        self.water = {e: {sname: 0 for sname in self.sem_names} for e in ENGINES}
        self.wait_targets = {sname: set() for sname in self.sem_names}

    def op(self, eng, emit, reads=(), writes=(), dma=False, tag=""):
        need = {}

        def _need(psem, pcnt):
            if psem == eng and eng == "pe":
                return
            if self.water[eng][psem] < pcnt:
                need[psem] = max(need.get(psem, 0), pcnt)

        if dma:
            lane = self.lane_i[eng]
            self.lane_i[eng] = (lane + 1) % self.NLANE
            sname = f"{eng}_l{lane}"
            inc = 16
            if self.cnt[sname] > 0:
                _need(sname, self.cnt[sname])
        else:
            sname = eng
            inc = 1
        for k in reads:
            w = self.writer.get(k)
            if w:
                _need(*w)
        for k in writes:
            w = self.writer.get(k)
            if w:
                _need(*w)
            for psem, pcnt in self.readers.get(k, {}).items():
                _need(psem, pcnt)
        waits = []
        for psem, pcnt in need.items():
            waits.append((psem, pcnt))
            self.water[eng][psem] = pcnt
            self.wait_targets[psem].add(pcnt)
        self.cnt[sname] += inc
        mycnt = self.cnt[sname]
        self.ops[eng].append((waits, emit, inc, sname, tag))
        for k in writes:
            self.writer[k] = (sname, mycnt)
            self.readers[k] = {}
        for k in reads:
            self.readers.setdefault(k, {})
            self.readers[k][sname] = mycnt
        return mycnt

    def fence(self, eng, old_keys, new_keys, tag="fence"):
        self.op(eng, lambda e: e.nop(), reads=old_keys,
                writes=list(old_keys) + list(new_keys), tag=tag)

    def keys_with_names(self, names):
        ks = set()
        for k in list(self.writer.keys()) + list(self.readers.keys()):
            if isinstance(k, tuple) and len(k) > 0 and k[0] in names:
                ks.add(k)
        return sorted(ks, key=repr)

    def emit(self, block, sems):
        handles = {"pe": block.tensor, "act": block.scalar, "dve": block.vector,
                   "sp": block.sync, "pool": block.gpsimd}
        for ename in ENGINES:
            ops = self.ops[ename]
            if not ops:
                continue

            def make_body(ename=ename, ops=ops):
                def body(eng):
                    pending = {}
                    running = {}
                    last_close = {}
                    guarded = {}

                    def guard(sn):
                        v = last_close.get(sn, 0)
                        if v and not guarded.get(sn) and v in self.wait_targets[sn]:
                            eng.wait_ge(sems[sn], v)
                            guarded[sn] = True

                    n = len(ops)
                    for i, (waits, emitf, inc, sname, tag) in enumerate(ops):
                        for (psem, pcnt) in waits:
                            eng.wait_ge(sems[psem], pcnt)
                            if pcnt >= last_close.get(psem, 0):
                                guarded[psem] = True
                        will_close = (inc == 16
                                      or running.get(sname, 0) + inc
                                      in self.wait_targets[sname]
                                      or i == n - 1)
                        if will_close:
                            guard(sname)
                        inst = emitf(eng)
                        running[sname] = running.get(sname, 0) + inc
                        pending[sname] = pending.get(sname, 0) + inc
                        if will_close:
                            if inst is None:
                                inst = eng.nop()
                            inst.then_inc(sems[sname], pending[sname])
                            pending[sname] = 0
                            last_close[sname] = running[sname]
                            guarded[sname] = False
                    left = {sn: p for sn, p in pending.items() if p}
                    for sn, p in left.items():
                        guard(sn)
                        eng.nop().then_inc(sems[sn], p)
                return body
            handles[ename](make_body())


def build_program(cfg, taps=()):
    NSEQ, S, D, DL, INNER, H = (cfg["NSEQ"], cfg["S"], cfg["D"], cfg["DL"],
                                cfg["INNER"], cfg["H"])
    T = NSEQ * S
    NC = D // 128
    NCL = DL // 128
    NCI = INNER // 128
    NPAIR = H // 2
    SPAN = min(512, S)
    NSPAN = S // SPAN
    NKC = S // 128
    NFS = T // SPAN
    ISCALE = 1.0 / float(np.sqrt(D))
    V65W = H * 65

    nc = bass.Bass("TRN2", target_bir_lowering=False)
    sch = Sched()

    x_h = nc.dram_tensor("x", [T, D], F32, kind="ExternalInput").ap()
    wq_h = nc.dram_tensor("W_q", [D, D], F32, kind="ExternalInput").ap()
    wk_h = nc.dram_tensor("W_k", [D, D], F32, kind="ExternalInput").ap()
    wv_h = nc.dram_tensor("W_v", [D, D], F32, kind="ExternalInput").ap()
    wo_h = nc.dram_tensor("W_o", [D, D], F32, kind="ExternalInput").ap()
    wdkv_h = nc.dram_tensor("W_down_kv", [D, DL], F32, kind="ExternalInput").ap()
    wupk_h = nc.dram_tensor("W_up_k", [DL, D], F32, kind="ExternalInput").ap()
    wupv_h = nc.dram_tensor("W_up_v", [DL, D], F32, kind="ExternalInput").ap()
    wg_h = nc.dram_tensor("w_gate", [INNER, D], F32, kind="ExternalInput").ap()
    wu_h = nc.dram_tensor("w_up", [INNER, D], F32, kind="ExternalInput").ap()
    wd_h = nc.dram_tensor("w_down", [D, INNER], F32, kind="ExternalInput").ap()
    out_h = nc.dram_tensor("out", [T, D], F32, kind="ExternalOutput").ap()

    wgt_h = nc.dram_tensor("wg_t", [D, INNER], BF16, kind="Internal").ap()
    wut_h = nc.dram_tensor("wu_t", [D, INNER], BF16, kind="Internal").ap()
    wdt_h = nc.dram_tensor("wd_t", [INNER, D], BF16, kind="Internal").ap()
    zdram_h = nc.dram_tensor("zdram", [H, SPAN], BF16, kind="Internal").ap()
    zrd_h = nc.dram_tensor("zrd", [H, SPAN], BF16, kind="Internal").ap()
    gdram_h = nc.dram_tensor("gdram", [4, 2048], F32, kind="Internal").ap()
    x1d_h = nc.dram_tensor("x1dram", [128, NSEQ * NC * S], BF16, kind="Internal").ap()
    DBG = cfg.get("DBG", False)
    if DBG:
        dbgA_h = nc.dram_tensor("dbgA", [128, 512], BF16, kind="ExternalOutput").ap()
        dbgB_h = nc.dram_tensor("dbgB", [128, 512], BF16, kind="ExternalOutput").ap()
        dbgC_h = nc.dram_tensor("dbgC", [128, 512], BF16, kind="ExternalOutput").ap()
        dbgD_h = nc.dram_tensor("dbgD", [128, 512], F32, kind="ExternalOutput").ap()
        dbgE_h = nc.dram_tensor("dbgE", [128, 512], BF16, kind="ExternalOutput").ap()
        dbgF_h = nc.dram_tensor("dbgF", [128, 512], F32, kind="ExternalOutput").ap()

    tap_h = {}
    for tname, pdim, fdim, tdt in taps:
        tap_h[tname] = nc.dram_tensor("tap_" + tname, [pdim, fdim], tdt,
                                      kind="ExternalOutput").ap()

    ctx = ExitStack()

    def sb(name, shape, dt, c=None):
        return (c or ctx).enter_context(nc.sbuf_tensor(name, shape, dt))

    def psalloc(name, shape, dt):
        return ctx.enter_context(nc.psum_tensor(name, shape, dt))

    cD = sb("cD", [128, S], BF16)
    cE = sb("cE", [128, S], BF16)
    cC = sb("cC", [128, 128], BF16)
    cId = sb("cId", [128, 128], BF16)
    ecolE = sb("ecolE", [128, NKC], F32)
    ecolD = sb("ecolD", [128, NKC], F32)
    ecol2 = sb("ecol2", [128, NKC], F32)
    cE2 = sb("cE2", [128, S], BF16)
    ones_bf = sb("ones_bf", [128, 1], BF16)
    onesrow = sb("onesrow", [1, 128], BF16)
    onesrow_f = sb("onesrow_f", [1, 128], F32)
    onescol_f = sb("onescol_f", [128, 1], F32)
    pairsel = sb("pairsel", [2, 128], BF16)
    wfa = sb("wfa", [128, 512], F32)
    wfb = sb("wfb", [128, 512], F32)
    wba = sb("wba", [128, 512], BF16)
    wbb = sb("wbb", [128, 512], BF16)
    wbc = sb("wbc", [128, 512], BF16)
    wbd = sb("wbd", [128, 512], BF16)
    wpa = sb("wpa", [128, 512], F32)
    wpb = sb("wpb", [128, 512], F32)
    wta = sb("wta", [128, 512], BF16)
    wtb = sb("wtb", [128, 512], BF16)
    wqA = sb("wqA", [128, 512], BF16)
    wqB = sb("wqB", [128, 512], BF16)
    gA = wta
    gB = wtb

    xstage = sb("xstage", [128, 512], F32)
    xstgb = sb("xstgb", [128, 512], BF16)
    scrA = sb("scrA", [128, 512], F32)
    scrI = sb("scrI", [128, 512], I16)
    scc = sb("scc", [128, 16], F32)
    r1 = sb("r1", [1, 512], F32)
    r2 = sb("r2", [1, 512], F32)
    rb1 = sb("rb1", [1, 512], BF16)
    rb2 = sb("rb2", [1, 512], BF16)
    bcA = sb("bcA", [128, 512], BF16)
    bcB = sb("bcB", [128, 512], BF16)
    thrcol = sb("thrcol", [128, 6], F32)
    mgcol = sb("mgcol", [128, 1], F32)
    colsumK = sb("colsumK", [128, NC], F32)
    zinb = sb("zinb", [128, (H * SPAN) // 128], BF16)
    epsc = sb("epsc", [1, 1], F32)
    zoutf = sb("zoutf", [128, (H * SPAN) // 128], BF16)
    zprb = sb("zprb", [2, SPAN], BF16)

    psP1 = psalloc("psP1", [128, 1024], F32)
    psP2 = psalloc("psP2", [128, 1024], F32)
    psQ1 = psalloc("psQ1", [128, 512], F32)
    psQ2 = psalloc("psQ2", [128, 512], F32)
    psM = psalloc("psM", [128, 512], F32)
    psB = psalloc("psB", [128, 1024], BF16)

    sems = {}

    def K(*a):
        return a

    def dve(emit, reads=(), writes=(), tag=""):
        sch.op("dve", emit, reads, writes, tag=tag)

    def act(emit, reads=(), writes=(), tag=""):
        sch.op("act", emit, reads, writes, tag=tag)

    def pe(emit, reads=(), writes=(), tag=""):
        sch.op("pe", emit, reads, writes, tag=tag)

    def dma(q, emit, reads=(), writes=(), tag=""):
        sch.op(q, emit, reads, writes, dma=True, tag=tag)

    def pool(emit, reads=(), writes=(), tag=""):
        sch.op("pool", emit, reads, writes, tag=tag)

    PS4 = [(psP1, 0, K("ps4", 0)), (psP1, 512, K("ps4", 1)),
           (psP2, 0, K("ps4", 2)), (psP2, 512, K("ps4", 3))]
    PK_P1 = [K("ps4", 0), K("ps4", 1)]
    PK_P2 = [K("ps4", 2), K("ps4", 3)]

    for off in range(0, S, 512):
        n = min(512, S - off)
        pool(lambda g, off=off, n=n: g.iota(
            scrA[:, 0:n], [[1, n]], base=off, channel_multiplier=0,
            allow_small_or_imprecise_dtypes=True),
            reads=[K("scrA")], writes=[K("scrA")], tag="iota")
        act(lambda s, off=off, n=n: s.activation(
            cD[:, off:off + n], scrA[:, 0:n], AF.Exp, scale=-LAM),
            reads=[K("scrA")], writes=[K("cD")], tag="mkD")
        act(lambda s, off=off, n=n: s.activation(
            cE[:, off:off + n], scrA[:, 0:n], AF.Exp, scale=LAM),
            reads=[K("scrA")], writes=[K("cE")], tag="mkE")
    for off in range(0, S, 512):
        n = min(512, S - off)
        pool(lambda g, off=off, n=n: g.iota(
            scrA[:, 0:n], [[1, n]], base=off, channel_multiplier=0,
            allow_small_or_imprecise_dtypes=True),
            reads=[K("scrA")], writes=[K("scrA")], tag="iota2")
        act(lambda s, off=off, n=n: s.activation(
            cE2[:, off:off + n], scrA[:, 0:n], AF.Exp, scale=2.0 * LAM),
            reads=[K("scrA")], writes=[K("cE2")], tag="mkE2")
    dve(lambda v: v.tensor_scalar(cD[:, :], cD[:, :], ISCALE, None, ALU.mult),
        reads=[K("cD")], writes=[K("cD")], tag="cDs")
    dve(lambda v: v.tensor_scalar(cE[:, :], cE[:, :], ISCALE, None, ALU.mult),
        reads=[K("cE")], writes=[K("cE")], tag="cEs")
    pool(lambda g: g.iota(scrA[:, 128:256], [[-1, 128]], base=0,
                          channel_multiplier=1,
                          allow_small_or_imprecise_dtypes=True),
         reads=[K("scrA")], writes=[K("scrA")], tag="delta")
    act(lambda s: s.activation(scrA[:, 256:384], scrA[:, 128:256], AF.Relu),
        reads=[K("scrA")], writes=[K("scrA")], tag="reluD")
    act(lambda s: s.activation(cC[:, :], scrA[:, 256:384], AF.Exp, scale=-2.0 * LAM),
        reads=[K("scrA")], writes=[K("cC")], tag="mkC")
    dve(lambda v: v.tensor_scalar(cId[:, :], scrA[:, 128:256], 0.0, None,
                                  ALU.is_equal),
        reads=[K("scrA")], writes=[K("cId")], tag="mkId")
    pool(lambda g: g.iota(scrA[:, 384:385], [[0, 1]], base=0, channel_multiplier=1,
                          allow_small_or_imprecise_dtypes=True),
         reads=[K("scrA")], writes=[K("scrA")], tag="pcol")
    act(lambda s: s.activation(scrA[:, 385:386], scrA[:, 384:385], AF.Exp,
                               scale=LAM),
        reads=[K("scrA")], writes=[K("scrA")], tag="ebE")
    act(lambda s: s.activation(scrA[:, 386:387], scrA[:, 384:385], AF.Exp,
                               scale=-LAM),
        reads=[K("scrA")], writes=[K("scrA")], tag="ebD")
    for kc in range(NKC):
        fE = float(np.exp(LAM * 128.0 * kc))
        fD = float(np.exp(-LAM * 128.0 * kc))
        dve(lambda v, kc=kc, fE=fE: v.tensor_scalar(
            ecolE[:, kc:kc + 1], scrA[:, 385:386], fE, None, ALU.mult),
            reads=[K("scrA")], writes=[K("ecolE")], tag="ecE")
        dve(lambda v, kc=kc, fD=fD: v.tensor_scalar(
            ecolD[:, kc:kc + 1], scrA[:, 386:387], fD, None, ALU.mult),
            reads=[K("scrA")], writes=[K("ecolD")], tag="ecD")
        dve(lambda v, kc=kc, fD=fD: v.tensor_scalar(
            ecol2[:, kc:kc + 1], ecolD[:, kc:kc + 1], ecolD[:, kc:kc + 1], None,
            ALU.mult),
            reads=[K("ecolD")], writes=[K("ecol2")], tag="ec2")
    dve(lambda v: v.memset(ones_bf[:, :], 1.0), writes=[K("ones_bf")], tag="c1")
    dve(lambda v: v.memset(epsc[:, :], EPS), writes=[K("epsc")], tag="ceps")
    dve(lambda v: v.memset(onesrow[:, :], 1.0), writes=[K("onesrow")], tag="c2")
    dve(lambda v: v.memset(onesrow_f[:, :], 1.0), writes=[K("onesrow_f")], tag="c3")
    dve(lambda v: v.memset(onescol_f[:, :], 1.0), writes=[K("onescol_f")], tag="c4")
    pool(lambda g: g.iota(scrA[0:2, 0:128], [[1, 128]], base=0,
                          channel_multiplier=-64,
                          allow_small_or_imprecise_dtypes=True),
         reads=[K("scrA")], writes=[K("scrA")], tag="c5i")
    dve(lambda v: v.tensor_scalar(scrA[0:2, 128:256], scrA[0:2, 0:128], 0.0, None,
                                  ALU.is_ge),
        reads=[K("scrA")], writes=[K("scrA")], tag="c5a")
    dve(lambda v: v.tensor_scalar(scrA[0:2, 0:128], scrA[0:2, 0:128], 63.0, None,
                                  ALU.is_le),
        reads=[K("scrA")], writes=[K("scrA")], tag="c5b")
    dve(lambda v: v.tensor_tensor(pairsel[:, :], scrA[0:2, 0:128],
                                  scrA[0:2, 128:256], ALU.mult),
        reads=[K("scrA")], writes=[K("pairsel")], tag="c5c")

    _sf = [0]
    _sbp = [0]
    _pf = [0]
    _pb = [0]

    def wbuf_f():
        i = _sf[0]; _sf[0] += 1
        return ((wfa, K("wfa")), (wfb, K("wfb")))[i % 2]

    def wbuf_b():
        i = _sbp[0]; _sbp[0] += 1
        return ((wba, K("wba")), (wbb, K("wbb")), (wbc, K("wbc")),
                (wbd, K("wbd")))[i % 4]

    PPF = deque([(wpa, K("wpa")), (wpb, K("wpb"))])
    PPB = [(wta, K("wta")), (wtb, K("wtb"))]
    PPS = [(wqB, K("wqB"))]
    _ps = [0]

    def pbuf_b():
        i = _pb[0]; _pb[0] += 1
        return PPB[i % len(PPB)]

    def pbuf_s():
        i = _ps[0]; _ps[0] += 1
        return PPS[i % len(PPS)]

    def wconvert(src_h, dst_h, rows, cols, q, extra=None, name="wc"):
        nch = rows // 128
        cw = min(512, cols)
        nst = cols // cw
        for i in range(nch):
            for j in range(nst):
                buf, bk = wbuf_f()
                bb, bbk = wbuf_b()
                dma(q, lambda e, i=i, j=j, buf=buf, src_h=src_h, cw=cw:
                    e.dma_start(buf[:, 0:cw],
                                src_h[128 * i:128 * i + 128, cw * j:cw * j + cw]),
                    reads=[bk], writes=[bk], tag=f"{name}ld")
                if extra is not None:
                    dma(q, lambda e, i=i, j=j, cw=cw, extra=extra:
                        e.dma_start(scrA[:, 0:cw],
                                    extra[128 * i:128 * i + 128, cw * j:cw * j + cw]),
                        reads=[K("scrA")], writes=[K("scrA")], tag=f"{name}l2")
                    dve(lambda v, buf=buf, bb=bb, cw=cw: v.tensor_tensor(
                        bb[:, 0:cw], buf[:, 0:cw], scrA[:, 0:cw], ALU.add),
                        reads=[bk, K("scrA")], writes=[bbk], tag=f"{name}ad")
                else:
                    dve(lambda v, buf=buf, bb=bb, cw=cw: v.tensor_copy(
                        bb[:, 0:cw], buf[:, 0:cw]),
                        reads=[bk], writes=[bbk], tag=f"{name}cs")
                dma(q, lambda e, i=i, j=j, bb=bb, dst_h=dst_h, cw=cw:
                    e.dma_start(dst_h[128 * i:128 * i + 128, cw * j:cw * j + cw],
                                bb[:, 0:cw]),
                    reads=[bbk], writes=[K(name, i, j)], tag=f"{name}st")

    def wstats(src_h, rows, cols, slot, want_m, q, name):
        nch = rows // 128
        nst = cols // 512
        first = [True]
        for i in range(nch):
            for j in range(nst):
                yield
                buf, bk = PPF.popleft()
                dma(q, lambda e, i=i, j=j, buf=buf, src_h=src_h:
                    e.dma_start(buf[:, :], src_h[128 * i:128 * i + 128,
                                                 512 * j:512 * j + 512]),
                    reads=[bk], writes=[bk], tag=f"{name}sl")
                if first[0]:
                    dve(lambda v, buf=buf: v.tensor_reduce(
                        scc[:, 0:1], buf[:, :], AX.X, ALU.add,
                        apply_absolute_value=True),
                        reads=[bk, K("scc0")], writes=[K("scc0")], tag=f"{name}r0")
                    first[0] = False
                else:
                    dve(lambda v, buf=buf: v.tensor_reduce(
                        scc[:, 1:2], buf[:, :], AX.X, ALU.add,
                        apply_absolute_value=True),
                        reads=[bk, K("scc1")], writes=[K("scc1")], tag=f"{name}r1")
                    dve(lambda v: v.tensor_tensor(scc[:, 0:1], scc[:, 0:1],
                                                  scc[:, 1:2], ALU.add),
                        reads=[K("scc0"), K("scc1")], writes=[K("scc0")],
                        tag=f"{name}ra")
                PPF.append((buf, bk))
        pe(lambda t: t.matmul(psM[0:1, 0:1], onescol_f[:, :], scc[:, 0:1],
                              start=True, stop=True),
           reads=[K("onescol_f"), K("scc0")], writes=[K("psM")], tag=f"{name}tm")
        nel = float(rows * cols)
        dve(lambda v: v.tensor_scalar(r1[0:1, 0:1], psM[0:1, 0:1], 1.0 / nel, 1e-5,
                                      ALU.mult, ALU.max),
            reads=[K("psM"), K("r1")], writes=[K("r1")], tag=f"{name}mn")
        dve(lambda v: v.tensor_scalar(r1[0:1, 1:2], r1[0:1, 0:1], 0.5, None, ALU.mult),
            reads=[K("r1")], writes=[K("r1")], tag=f"{name}th")
        pe(lambda t: t.matmul(psM[0:128, 1:2], onesrow_f[:, :], r1[0:1, 1:2],
                              start=True, stop=True),
           reads=[K("onesrow_f"), K("r1")], writes=[K("psM")], tag=f"{name}bm")
        dve(lambda v, slot=slot: v.tensor_copy(thrcol[:, slot:slot + 1], psM[:, 1:2]),
            reads=[K("psM"), K("thr")], writes=[K("thr")], tag=f"{name}tc")
        dve(lambda v, slot=slot: v.tensor_scalar(
            thrcol[:, slot + 1:slot + 2], thrcol[:, slot:slot + 1], -1.0, None,
            ALU.mult),
            reads=[K("thr")], writes=[K("thr")], tag=f"{name}nt")
        if want_m:
            pe(lambda t: t.matmul(psM[0:128, 2:3], onesrow_f[:, :], r1[0:1, 0:1],
                                  start=True, stop=True),
               reads=[K("onesrow_f"), K("r1")], writes=[K("psM")], tag=f"{name}m2")
            dve(lambda v: v.tensor_copy(mgcol[:, :], psM[:, 2:3]),
                reads=[K("psM")], writes=[K("mgcol")], tag=f"{name}mc")

    def wquant(src_h, dst_h, rows, cols, slot, q, name, colsum=False):
        nch = rows // 128
        nst = cols // 512
        idx = [(i, j) for i in range(nch) for j in range(nst)]
        pend = deque()
        li = [0]

        def emit_load():
            if li[0] < len(idx) and PPF:
                i, j = idx[li[0]]
                li[0] += 1
                buf, bk = PPF.popleft()
                dma(q, lambda e, i=i, j=j, buf=buf, src_h=src_h:
                    e.dma_start(buf[:, :], src_h[128 * i:128 * i + 128,
                                                 512 * j:512 * j + 512]),
                    reads=[bk], writes=[bk], tag=f"{name}ql")
                pend.append((i, j, buf, bk))

        for _step in range(len(idx)):
            yield
            while li[0] < len(idx) and PPF:
                emit_load()
            i, j, buf, bk = pend.popleft()
            tb, tbk = pbuf_b()
            dve(lambda v, buf=buf, slot=slot: v.tensor_scalar(
                wqA[:, :], buf[:, :], thrcol[:, slot:slot + 1], None, ALU.is_ge),
                reads=[bk, K("thr"), K("wqA")], writes=[K("wqA")],
                tag=f"{name}ge")
            dve(lambda v, buf=buf, slot=slot: v.tensor_scalar(
                wqB[:, :], buf[:, :], thrcol[:, slot + 1:slot + 2], None,
                ALU.is_le),
                reads=[bk, K("thr"), K("wqB")], writes=[K("wqB")],
                tag=f"{name}le")
            PPF.append((buf, bk))
            dve(lambda v, tb=tb: v.tensor_tensor(tb[:, :], wqA[:, :],
                                                 wqB[:, :], ALU.subtract),
                reads=[K("wqA"), K("wqB")], writes=[tbk], tag=f"{name}tn")
            if colsum:
                dve(lambda v, tb=tb: v.tensor_reduce(scc[:, 2:3], tb[:, :],
                                                     AX.X, ALU.add),
                    reads=[tbk, K("scc2")], writes=[K("scc2")], tag=f"{name}cs")
                if j == 0:
                    dve(lambda v, i=i: v.tensor_copy(colsumK[:, i:i + 1],
                                                     scc[:, 2:3]),
                        reads=[K("scc2"), K("csK")], writes=[K("csK")],
                        tag=f"{name}c0")
                else:
                    dve(lambda v, i=i: v.tensor_tensor(
                        colsumK[:, i:i + 1], colsumK[:, i:i + 1], scc[:, 2:3],
                        ALU.add),
                        reads=[K("scc2"), K("csK")], writes=[K("csK")],
                        tag=f"{name}ca")
            for b in range(4):
                pe(lambda t, tb=tb, b=b: t.transpose(
                    psB[:, 512 + 128 * b:512 + 128 * b + 128],
                    tb[:, 128 * b:128 * b + 128], cId[:, :]),
                   reads=[tbk, K("cId")], writes=[K("psB2")], tag=f"{name}tr")
            stg, sk = pbuf_s()
            act(lambda s, stg=stg: s.activation(stg[:, 0:512],
                                                psB[:, 512:1024], AF.Copy),
                reads=[K("psB2"), sk], writes=[sk], tag=f"{name}ev")

            def emst(e, i=i, j=j, stg=stg):
                d4 = dst_h.rearrange("(a p) c -> a p c", p=128)
                dd = d4[4 * j:4 * j + 4, :, 128 * i:128 * i + 128]
                dd = dd.rearrange("a p o -> p a o")
                src = stg[:, 0:512].rearrange("p (a o) -> p a o", o=128)
                return e.dma_start(dd, src)
            dma(q, emst, reads=[sk], writes=[K(name, "q", i, j)],
                tag=f"{name}qs")

    def wq_strip_keys(name, ci, og4):
        return [K(name, "q", 4 * og4 + r, ci // 4) for r in range(4)]

    class Kit:
        def __init__(self, **kw):
            self.__dict__.update(kw)

    KIT0 = Kit(psS=psM, psSk=K("psM"), psQ=psQ1, psQk=K("psQ1"), single=False,
               r1=r1, r1k=K("r1"), r2=r2, r2k=K("r2"),
               rb1=rb1, rb1k=K("rb1"), rb2=rb2, rb2k=K("rb2"),
               bcA=bcA, bcAk=K("bcA"), bcB=bcB, bcBk=K("bcB"),
               xs=xstgb, xsk=K("xstgb"), m2=scrA, m2k=K("scrA"))

    def ln_stats(src_ap_fn, src_keys, uid, kt=None):
        kt = kt or KIT0
        ln_stats_a(src_ap_fn, src_keys, uid, kt)
        ln_stats_b(uid, kt)

    def ln_stats_a(src_ap_fn, src_keys, uid, kt=None):
        kt = kt or KIT0
        for c in range(NC):
            pe(lambda t, c=c: t.matmul(kt.psS[0:1, 0:SPAN], ones_bf[:, :],
                                       src_ap_fn(c), start=(c == 0),
                                       stop=(c == NC - 1)),
               reads=[K("ones_bf")] + src_keys(c), writes=[kt.psSk],
               tag=f"{uid}sm")
        if kt.single:
            dve(lambda v: v.tensor_scalar(kt.r1[0:1, 0:SPAN],
                                          kt.psS[0:1, 0:SPAN], 1.0 / D,
                                          None, ALU.mult),
                reads=[kt.psSk, kt.r1k], writes=[kt.r1k], tag=f"{uid}m")
        for c in range(NC):
            act(lambda s, c=c: s.activation(kt.xs[:, 0:SPAN], src_ap_fn(c),
                                            AF.Square),
                reads=src_keys(c) + [kt.xsk], writes=[kt.xsk],
                tag=f"{uid}sq")
            pe(lambda t, c=c: t.matmul(kt.psQ[0:1, 0:SPAN], ones_bf[:, :],
                                       kt.xs[:, 0:SPAN], start=(c == 0),
                                       stop=(c == NC - 1)),
               reads=[K("ones_bf"), kt.xsk] + ([kt.r1k] if kt.single else []),
               writes=[kt.psQk],
               tag=f"{uid}ss")
        if not kt.single:
            dve(lambda v: v.tensor_scalar(kt.r1[0:1, 0:SPAN],
                                          kt.psS[0:1, 0:SPAN], 1.0 / D,
                                          None, ALU.mult),
                reads=[kt.psSk, kt.r1k], writes=[kt.r1k], tag=f"{uid}m")
        dve(lambda v: v.tensor_scalar(kt.r2[0:1, 0:SPAN], kt.psQ[0:1, 0:SPAN],
                                      1.0 / D, None, ALU.mult),
            reads=[kt.psQk, kt.r2k], writes=[kt.r2k], tag=f"{uid}q")
        dve(lambda v: v.tensor_tensor(kt.m2[0:1, 0:SPAN], kt.r1[0:1, 0:SPAN],
                                      kt.r1[0:1, 0:SPAN], ALU.mult),
            reads=[kt.r1k, kt.m2k], writes=[kt.m2k], tag=f"{uid}m2")
        dve(lambda v: v.tensor_tensor(kt.r2[0:1, 0:SPAN], kt.r2[0:1, 0:SPAN],
                                      kt.m2[0:1, 0:SPAN], ALU.subtract),
            reads=[kt.r2k, kt.m2k], writes=[kt.r2k], tag=f"{uid}vr")
        act(lambda s: s.activation(kt.r2[0:1, 0:SPAN], kt.r2[0:1, 0:SPAN],
                                   AF.Ln, bias=epsc[0:1, 0:1]),
            reads=[kt.r2k, K("epsc")], writes=[kt.r2k], tag=f"{uid}ln")
        act(lambda s: s.activation(kt.r2[0:1, 0:SPAN], kt.r2[0:1, 0:SPAN],
                                   AF.Exp, scale=-0.5),
            reads=[kt.r2k], writes=[kt.r2k], tag=f"{uid}rs")
        dve(lambda v: v.tensor_tensor(kt.r1[0:1, 0:SPAN], kt.r1[0:1, 0:SPAN],
                                      kt.r2[0:1, 0:SPAN], ALU.mult),
            reads=[kt.r1k, kt.r2k], writes=[kt.r1k], tag=f"{uid}mr")

    def ln_stats_b(uid, kt=None):
        kt = kt or KIT0
        pe(lambda t: t.matmul(kt.psS[:, 0:SPAN], onesrow_f[:, :],
                              kt.r2[0:1, 0:SPAN], start=True, stop=True),
           reads=[K("onesrow_f"), kt.r2k], writes=[kt.psSk], tag=f"{uid}b1")
        act(lambda a: a.activation(kt.bcA[:, 0:SPAN], kt.psS[:, 0:SPAN],
                                   AF.Copy),
            reads=[kt.psSk, kt.bcAk], writes=[kt.bcAk], tag=f"{uid}e1")
        pe(lambda t: t.matmul(kt.psS[:, 0:SPAN], onesrow_f[:, :],
                              kt.r1[0:1, 0:SPAN], start=True, stop=True),
           reads=[K("onesrow_f"), kt.r1k], writes=[kt.psSk], tag=f"{uid}b2")
        act(lambda a: a.activation(kt.bcB[:, 0:SPAN], kt.psS[:, 0:SPAN],
                                   AF.Copy),
            reads=[kt.psSk, kt.bcBk], writes=[kt.bcBk], tag=f"{uid}e2")

    def _prep_gen():
        yield from wstats(wg_h, INNER, D, 0, True, "pool", "wg")
        yield from wstats(wu_h, INNER, D, 2, False, "pool", "wu")
        yield from wstats(wd_h, D, INNER, 4, False, "pool", "wd")
        yield from wquant(wg_h, wgt_h, INNER, D, 0, "pool", "wg")
        yield from wquant(wu_h, wut_h, INNER, D, 2, "pool", "wu")
        yield from wquant(wd_h, wdt_h, D, INNER, 4, "pool", "wd", colsum=True)
        dve(lambda v: v.tensor_scalar(colsumK[:, 0:NC], colsumK[:, 0:NC], KOFF,
                                      None, ALU.mult),
            reads=[K("csK")], writes=[K("csK")], tag="csKs")
    _prep = _prep_gen()

    def pump(n=2):
        for _ in range(n):
            if next(_prep, "done") == "done":
                break

    seq_ctx = ExitStack()
    x1 = sb("x1sb", [128, NC * S], BF16, seq_ctx)
    q_sb = sb("q_sb", [128, NC * S], BF16, seq_ctx)
    krec_sb = sb("krec_sb", [128, NC * S], BF16, seq_ctx)
    v65 = sb("v65", [128, NKC * V65W], BF16, seq_ctx)
    ao0 = sb("ao0", [128, NC * SPAN], BF16, seq_ctx)
    ao1 = sb("ao1", [128, NC * SPAN], BF16, seq_ctx)
    zst = sb("zst", [65, SPAN], BF16, seq_ctx)
    zst2 = sb("zst2", [65, SPAN], BF16, seq_ctx)
    LATK = ["lat", "xln"]
    SCOREK = ["qsl", "pp", "pj"]

    def x1ap(c, lt, n):
        return x1[:, c * S + lt: c * S + lt + n]

    def x1k(c, lt):
        return K("x1", c, lt // SPAN)

    def wstrip_mm(wdram, wdram_keys, NI, NO, rhs_fn, rhs_keys, uid, evac_fn,
                  strip_w=512, cast=False, extra=None):
        npg = strip_w // 128
        for og4 in range(NO // npg):
            for ci in range(NI):
                wb, wk = wbuf_b()
                if cast:
                    f1, fk1 = wbuf_f()
                    dma("sp", lambda e, og4=og4, ci=ci, f1=f1, wdram=wdram,
                        strip_w=strip_w:
                        e.dma_start(f1[:, 0:strip_w],
                                    wdram[128 * ci:128 * ci + 128,
                                          strip_w * og4:strip_w * og4 + strip_w]),
                        reads=[fk1] + wdram_keys(ci, og4), writes=[fk1],
                        tag=f"{uid}wl")
                    if extra is not None:
                        f2, fk2 = wbuf_f()
                        dma("sp", lambda e, og4=og4, ci=ci, f2=f2, extra=extra,
                            strip_w=strip_w:
                            e.dma_start(f2[:, 0:strip_w],
                                        extra[128 * ci:128 * ci + 128,
                                              strip_w * og4:strip_w * og4 + strip_w]),
                            reads=[fk2], writes=[fk2], tag=f"{uid}w2")
                        dve(lambda v, f1=f1, f2=f2, wb=wb, strip_w=strip_w:
                            v.tensor_tensor(wb[:, 0:strip_w], f1[:, 0:strip_w],
                                            f2[:, 0:strip_w], ALU.add),
                            reads=[fk1, fk2, wk], writes=[wk], tag=f"{uid}wa")
                    else:
                        act(lambda a, f1=f1, wb=wb, strip_w=strip_w:
                            a.activation(wb[:, 0:strip_w], f1[:, 0:strip_w],
                                         AF.Copy),
                            reads=[fk1, wk], writes=[wk], tag=f"{uid}wc")
                else:
                    dma("sp", lambda e, og4=og4, ci=ci, wb=wb, wdram=wdram,
                        strip_w=strip_w:
                        e.dma_start(wb[:, 0:strip_w],
                                    wdram[128 * ci:128 * ci + 128,
                                          strip_w * og4:strip_w * og4 + strip_w]),
                        reads=[wk] + wdram_keys(ci, og4), writes=[wk],
                        tag=f"{uid}wl")
                for jj in range(npg):
                    pt, poff, pk = PS4[jj]
                    pe(lambda t, ci=ci, jj=jj, wb=wb, pt=pt, poff=poff:
                       t.matmul(pt[0:128, poff:poff + SPAN],
                                wb[:, 128 * jj:128 * jj + 128], rhs_fn(ci),
                                start=(ci == 0), stop=(ci == NI - 1)),
                       reads=[wk] + rhs_keys(ci), writes=[pk], tag=f"{uid}mm")
            for jj in range(npg):
                evac_fn(og4 * npg + jj, jj)

    def seq_run(s_i):
        t0 = s_i * S
        phase_ctx = ExitStack()
        lat_sb = sb("lat_sb%d" % s_i, [128, NCL * S], BF16, phase_ctx)
        xln = sb("xln%d" % s_i, [128, NC * SPAN], BF16, phase_ctx)
        sch.fence("dve", sch.keys_with_names(SCOREK),
                  [K("lat", c, sp) for c in range(NCL) for sp in range(NSPAN)] +
                  [K("xln", c) for c in range(NC)],
                  tag="fence_lat")

        for tt in range(S // 128):
            tg = t0 + tt * 128
            lt = tt * 128
            for g in range(2):
                dma("sp", lambda e, tg=tg, g=g: e.dma_start(
                    xstage[:, 0:512], x_h[tg:tg + 128, 512 * g:512 * g + 512]),
                    reads=[K("xstage")], writes=[K("xstage")], tag="xld")
                act(lambda a: a.activation(xstgb[:, 0:512], xstage[:, 0:512],
                                           AF.Copy),
                    reads=[K("xstage"), K("xstgb")], writes=[K("xstgb")], tag="xcs")
                for b in range(4):
                    pe(lambda t, b=b: t.transpose(
                        psB[:, 128 * b:128 * b + 128],
                        xstgb[:, 128 * b:128 * b + 128], cId[:, :]),
                       reads=[K("xstgb"), K("cId")], writes=[K("psB")], tag="xtr")
                for b in range(4):
                    c = 4 * g + b
                    dve(lambda v, c=c, b=b, lt=lt: v.tensor_copy(
                        x1ap(c, lt, 128), psB[:, 128 * b:128 * b + 128]),
                        reads=[K("psB")], writes=[x1k(c, lt)], tag="xev")

        for sp in range(NSPAN):
            lt0 = sp * SPAN
            ln_stats(lambda c, lt0=lt0: x1ap(c, lt0, SPAN),
                     lambda c, lt0=lt0: [x1k(c, lt0)], f"l1_{s_i}_{sp}_")
            for c in range(NC):
                dve(lambda v, c=c, lt0=lt0: v.tensor_tensor(
                    xln[:, c * SPAN:(c + 1) * SPAN], x1ap(c, lt0, SPAN),
                    bcA[:, 0:SPAN], ALU.mult),
                    reads=[x1k(c, lt0), K("bcA"), K("xln", c)],
                    writes=[K("xln", c)], tag="l1a")
                dve(lambda v, c=c: v.tensor_tensor(
                    xln[:, c * SPAN:(c + 1) * SPAN],
                    xln[:, c * SPAN:(c + 1) * SPAN], bcB[:, 0:SPAN], ALU.subtract),
                    reads=[K("xln", c), K("bcB")], writes=[K("xln", c)], tag="l1b")

            pump(3)

            def evq(og, jj, sp=sp):
                pt, poff, pk = PS4[jj]
                dve(lambda v, og=og, pt=pt, poff=poff, sp=sp: v.tensor_copy(
                    q_sb[:, og * S + sp * SPAN: og * S + sp * SPAN + SPAN],
                    pt[0:128, poff:poff + SPAN]),
                    reads=[pk, K("q", og, sp)], writes=[K("q", og, sp)], tag="qev")
            wstrip_mm(wq_h, lambda ci, og4: [], NC, NC,
                      lambda ci: xln[:, ci * SPAN:(ci + 1) * SPAN],
                      lambda ci: [K("xln", ci)], f"q{s_i}{sp}", evq, cast=True)

            def evkv(og, jj, sp=sp):
                pt, poff, pk = PS4[jj]
                dve(lambda v, og=og, pt=pt, poff=poff, sp=sp: v.tensor_copy(
                    krec_sb[:, og * S + sp * SPAN: og * S + sp * SPAN + SPAN],
                    pt[0:128, poff:poff + SPAN]),
                    reads=[pk, K("kr", og, sp)], writes=[K("kr", og, sp)],
                    tag="kvev")
            wstrip_mm(wk_h, lambda ci, og4: [], NC, NC,
                      lambda ci: xln[:, ci * SPAN:(ci + 1) * SPAN],
                      lambda ci: [K("xln", ci)], f"kv{s_i}{sp}", evkv, cast=True,
                      extra=wv_h)

        for sp in range(NSPAN):
            for ci in range(NC):
                wb, wk = wbuf_b()
                f1, fk1 = wbuf_f()
                dma("sp", lambda e, ci=ci, f1=f1:
                    e.dma_start(f1[:, 0:DL], wdkv_h[128 * ci:128 * ci + 128, 0:DL]),
                    reads=[fk1], writes=[fk1], tag="ltwl")
                act(lambda a, f1=f1, wb=wb: a.activation(wb[:, 0:DL],
                                                         f1[:, 0:DL], AF.Copy),
                    reads=[fk1, wk], writes=[wk], tag="ltwc")
                for jj in range(NCL):
                    pt, poff, pk = PS4[jj]
                    pe(lambda t, ci=ci, jj=jj, wb=wb, pt=pt, poff=poff, sp=sp:
                       t.matmul(pt[0:128, poff:poff + SPAN],
                                wb[:, 128 * jj:128 * jj + 128],
                                krec_sb[:, ci * S + sp * SPAN:
                                        ci * S + sp * SPAN + SPAN],
                                start=(ci == 0), stop=(ci == NC - 1)),
                       reads=[wk, K("kr", ci, sp)], writes=[pk], tag="ltmm")
            for jj in range(NCL):
                pt, poff, pk = PS4[jj]
                dve(lambda v, jj=jj, pt=pt, poff=poff, sp=sp: v.tensor_copy(
                    lat_sb[:, jj * S + sp * SPAN: jj * S + sp * SPAN + SPAN],
                    pt[0:128, poff:poff + SPAN]),
                    reads=[pk, K("lat", jj, sp)], writes=[K("lat", jj, sp)],
                    tag="ltev")

        for sp in range(NSPAN):
            def evkr(og, jj, sp=sp):
                pt, poff, pk = PS4[jj]
                dve(lambda v, og=og, pt=pt, poff=poff, sp=sp: v.tensor_copy(
                    krec_sb[:, og * S + sp * SPAN: og * S + sp * SPAN + SPAN],
                    pt[0:128, poff:poff + SPAN]),
                    reads=[pk, K("kr", og, sp)], writes=[K("kr", og, sp)],
                    tag="krev")
            wstrip_mm(wupk_h, lambda ci, og4: [], NCL, NC,
                      lambda ci, sp=sp: lat_sb[:, ci * S + sp * SPAN:
                                               ci * S + sp * SPAN + SPAN],
                      lambda ci, sp=sp: [K("lat", ci, sp)], f"kr{s_i}{sp}", evkr,
                      cast=True)

        for sp in range(NSPAN):
            def evvr(og, jj, sp=sp):
                pt, poff, pk = PS4[jj]
                dve(lambda v, pt=pt, poff=poff: v.tensor_copy(
                    xstgb[:, 0:SPAN], pt[0:128, poff:poff + SPAN]),
                    reads=[pk, K("xstgb")], writes=[K("xstgb")], tag="vev")
                for b in range(SPAN // 128):
                    kc = sp * (SPAN // 128) + b
                    pe(lambda t, b=b: t.transpose(
                        psB[:, 0:128], xstgb[:, 128 * b:128 * b + 128], cId[:, :]),
                       reads=[K("xstgb"), K("cId")], writes=[K("psB")], tag="vtr")
                    for hh in range(2):
                        hidx = 2 * og + hh
                        dve(lambda v, kc=kc, hidx=hidx, hh=hh: v.tensor_copy(
                            v65[:, kc * V65W + hidx * 65:
                                 kc * V65W + hidx * 65 + 64],
                            psB[:, 64 * hh:64 * hh + 64]),
                            reads=[K("psB"), K("v65", kc)], writes=[K("v65", kc)],
                            tag="v65e")
            wstrip_mm(wupv_h, lambda ci, og4: [], NCL, NC,
                      lambda ci, sp=sp: lat_sb[:, ci * S + sp * SPAN:
                                               ci * S + sp * SPAN + SPAN],
                      lambda ci, sp=sp: [K("lat", ci, sp)], f"vr{s_i}{sp}", evvr,
                      cast=True)
        for kc in range(NKC):
            def emones(v, kc=kc):
                apv = bass.AP(v65, kc * V65W + 64,
                              [[NKC * V65W, 128], [65, H], [1, 1]])
                return v.memset(apv, 1.0)
            dve(emones, reads=[K("v65", kc)], writes=[K("v65", kc)], tag="v65o")

        phase_ctx.close()
        phase_ctx2 = ExitStack()
        qsl = sb("qsl%d" % s_i, [128, 2 * SPAN], BF16, phase_ctx2)
        pp = sb("pp%d" % s_i, [128, 2048], BF16, phase_ctx2)
        pj = sb("pj%d" % s_i, [128, NC * SPAN], BF16, phase_ctx2)
        sch.fence("dve", sch.keys_with_names(LATK),
                  [K("qsl"), K("pp", 0), K("pp", 1)] +
                  [K("pj", c) for c in range(NC)], tag="fence_score")

        def scores_span(sp):
            aob = ao0 if sp % 2 == 0 else ao1
            aokey = K("ao", sp % 2)
            q0 = sp * SPAN
            for pair in range(NPAIR):
                pump(2)
                dve(lambda v, pair=pair, q0=q0: v.tensor_tensor(
                    qsl[:, 0:SPAN], q_sb[:, pair * S + q0: pair * S + q0 + SPAN],
                    cD[:, q0:q0 + SPAN], ALU.mult),
                    reads=[K("q", pair, sp), K("cD"), K("qsl")], writes=[K("qsl")],
                    tag="qsD")
                dve(lambda v, pair=pair, q0=q0: v.tensor_tensor(
                    qsl[:, SPAN:2 * SPAN],
                    q_sb[:, pair * S + q0: pair * S + q0 + SPAN],
                    cE[:, q0:q0 + SPAN], ALU.mult),
                    reads=[K("q", pair, sp), K("cE"), K("qsl")], writes=[K("qsl")],
                    tag="qsE")
                def emit_qk(kc, pair=pair, q0=q0):
                    pbuf = psP1 if kc % 2 == 0 else psP2
                    pks = PK_P1 if kc % 2 == 0 else PK_P2
                    kc0 = kc * 128
                    kkey = K("kr", pair, (kc * 128) // SPAN)
                    hi_all = kc0 > q0 + SPAN - 1
                    qoff = SPAN if hi_all else 0
                    for hh in range(2):
                        lo = 64 * hh
                        po = 512 * hh
                        pe(lambda t, pair=pair, kc=kc, pbuf=pbuf, po=po,
                           qoff=qoff, lo=lo:
                           t.matmul(pbuf[0:128, po:po + SPAN],
                                    krec_sb[lo:lo + 64,
                                            pair * S + kc * 128:
                                            pair * S + kc * 128 + 128],
                                    qsl[lo:lo + 64, qoff:qoff + SPAN],
                                    start=True, stop=True),
                           reads=[kkey, K("qsl")], writes=pks, tag="qk")

                emit_qk(0)
                for kc in range(NKC):
                    pbuf = psP1 if kc % 2 == 0 else psP2
                    pks = PK_P1 if kc % 2 == 0 else PK_P2
                    ppo = (kc % 2) * 1024
                    ppk = K("pp", kc % 2)
                    kc0 = kc * 128
                    hi_all = kc0 > q0 + SPAN - 1
                    mixed = (not hi_all) and (kc0 + 127 >= q0)
                    if mixed:
                        qbd = (kc0 - q0) // 128
                        for hh in range(2):
                            po = 512 * hh
                            if qbd > 0:
                                dve(lambda v, pbuf=pbuf, po=po, qbd=qbd, q0=q0, kc=kc:
                                    v.scalar_tensor_tensor(
                                        pbuf[:, po:po + 128 * qbd],
                                        pbuf[:, po:po + 128 * qbd],
                                        ecol2[:, kc:kc + 1],
                                        cE2[:, q0:q0 + 128 * qbd],
                                        ALU.mult, ALU.mult),
                                    reads=pks + [K("ecol2"), K("cE2")], writes=pks,
                                    tag="ufix")
                            dve(lambda v, pbuf=pbuf, po=po, qbd=qbd: v.tensor_tensor(
                                pbuf[:, po + 128 * qbd:po + 128 * qbd + 128],
                                pbuf[:, po + 128 * qbd:po + 128 * qbd + 128],
                                cC[:, :], ALU.mult),
                                reads=pks + [K("cC")], writes=pks, tag="cfx")
                    ecol = ecolD if hi_all else ecolE
                    eck = K("ecolD") if hi_all else K("ecolE")
                    if SPAN == 512:
                        act(lambda s, pbuf=pbuf, ppo=ppo, kc=kc, ecol=ecol:
                            s.activation(pp[:, ppo:ppo + 1024], pbuf[:, :],
                                         AF.Exp, scale=ecol[:, kc:kc + 1]),
                            reads=pks + [eck, ppk], writes=[ppk], tag="ex")
                    else:
                        for hh in range(2):
                            po = 512 * hh
                            act(lambda s, pbuf=pbuf, ppo=ppo, po=po, kc=kc,
                                ecol=ecol:
                                s.activation(pp[:, ppo + po:ppo + po + SPAN],
                                             pbuf[:, po:po + SPAN], AF.Exp,
                                             scale=ecol[:, kc:kc + 1]),
                                reads=pks + [eck, ppk], writes=[ppk], tag="ex2")
                    if kc + 1 < NKC:
                        emit_qk(kc + 1)
                    for hh in range(2):
                        hidx = 2 * pair + hh
                        pq = psQ1 if hh == 0 else psQ2
                        pqk = K("psQ1") if hh == 0 else K("psQ2")
                        pe(lambda t, kc=kc, hidx=hidx, pq=pq, ppo=ppo, hh=hh:
                           t.matmul(pq[0:65, 0:SPAN],
                                    v65[:, kc * V65W + hidx * 65:
                                         kc * V65W + hidx * 65 + 65],
                                    pp[:, ppo + 512 * hh:ppo + 512 * hh + SPAN],
                                    start=(kc == 0), stop=(kc == NKC - 1)),
                           reads=[K("v65", kc), ppk], writes=[pqk], tag="pv")
                for hh in range(2):
                    pq = psQ1 if hh == 0 else psQ2
                    pqk = K("psQ1") if hh == 0 else K("psQ2")
                    zt = zst if hh == 0 else zst2
                    ztk = K("zst", hh)
                    dve(lambda v, pair=pair, hh=hh, pq=pq, aob=aob: v.tensor_copy(
                        aob[64 * hh:64 * hh + 64, pair * SPAN:(pair + 1) * SPAN],
                        pq[0:64, 0:SPAN]),
                        reads=[pqk, aokey], writes=[aokey], tag="aoe")
                    dve(lambda v, pq=pq, zt=zt: v.tensor_copy(
                        zt[64:65, 0:SPAN], pq[64:65, 0:SPAN]),
                        reads=[pqk, ztk], writes=[ztk], tag="zev")
                    zw0 = (H * SPAN) // 128
                    dma("sp", lambda e, pair=pair, hh=hh, zt=zt, zw0=zw0:
                        e.dma_start(
                            zinb[8 * (2 * pair + hh):8 * (2 * pair + hh) + 8,
                                 0:zw0],
                            zt[64:65, 0:SPAN]),
                        reads=[ztk], writes=[K("zd", pair, hh)], tag="zs")
            zw = (H * SPAN) // 128

            def _zr(v, zw=zw):
                with nc.allow_low_precision(reason="1/Z stored bf16 as before"):
                    return v.reciprocal(zoutf[:, 0:zw], zinb[:, 0:zw])
            dve(_zr,
                reads=[K("zd", p2, h2) for p2 in range(NPAIR)
                       for h2 in range(2)] + [K("zinb"), K("zoutf")],
                writes=[K("zoutf")], tag="zr")
            for pair in range(NPAIR):
                dma("sp", lambda e, pair=pair, zw=zw: e.dma_start(
                    zprb[:, 0:SPAN], zoutf[16 * pair:16 * pair + 16, 0:zw]),
                    reads=[K("zoutf"), K("zprb")], writes=[K("zprb")], tag="zp")
                pe(lambda t: t.matmul(psM[:, 0:SPAN], pairsel[:, :],
                                      zprb[:, 0:SPAN], start=True, stop=True),
                   reads=[K("pairsel"), K("zprb")], writes=[K("psM")], tag="zb")
                dve(lambda v: v.tensor_copy(xstgb[:, 0:SPAN], psM[:, 0:SPAN]),
                    reads=[K("psM"), K("xstgb")], writes=[K("xstgb")], tag="zbe")
                dve(lambda v, pair=pair, aob=aob: v.tensor_tensor(
                    aob[:, pair * SPAN:(pair + 1) * SPAN],
                    aob[:, pair * SPAN:(pair + 1) * SPAN], xstgb[:, 0:SPAN],
                    ALU.mult),
                    reads=[aokey, K("xstgb")], writes=[aokey], tag="zap")

        def finish_span(sp):
            aob = ao0 if sp % 2 == 0 else ao1
            aokey = K("ao", sp % 2)
            lt0 = sp * SPAN

            def evo(og, jj):
                pt, poff, pk = PS4[jj]
                dve(lambda v, og=og, pt=pt, poff=poff: v.tensor_copy(
                    pj[:, og * SPAN:(og + 1) * SPAN], pt[0:128, poff:poff + SPAN]),
                    reads=[pk, K("pj", og)], writes=[K("pj", og)], tag="woe")
            wstrip_mm(wo_h, lambda ci, og4: [], NC, NC,
                      lambda ci: aob[:, ci * SPAN:(ci + 1) * SPAN],
                      lambda ci: [aokey], f"wo{s_i}{sp}", evo, cast=True)
            ln_stats(lambda c: pj[:, c * SPAN:(c + 1) * SPAN],
                     lambda c: [K("pj", c)], f"l2a_{s_i}_{sp}_")
            for c in range(NC):
                dve(lambda v, c=c: v.tensor_tensor(
                    pj[:, c * SPAN:(c + 1) * SPAN], pj[:, c * SPAN:(c + 1) * SPAN],
                    bcA[:, 0:SPAN], ALU.mult),
                    reads=[K("pj", c), K("bcA")], writes=[K("pj", c)], tag="l2m")
                dve(lambda v, c=c: v.tensor_tensor(
                    pj[:, c * SPAN:(c + 1) * SPAN], pj[:, c * SPAN:(c + 1) * SPAN],
                    bcB[:, 0:SPAN], ALU.subtract),
                    reads=[K("pj", c), K("bcB")], writes=[K("pj", c)], tag="l2s")
                dve(lambda v, c=c, lt0=lt0: v.tensor_tensor(
                    x1ap(c, lt0, SPAN), x1ap(c, lt0, SPAN),
                    pj[:, c * SPAN:(c + 1) * SPAN], ALU.add),
                    reads=[K("pj", c), x1k(c, lt0)], writes=[x1k(c, lt0)],
                    tag="res1")

        for sp in range(NSPAN):
            scores_span(sp)
            pump(4)
            if sp >= 1:
                finish_span(sp - 1)
        finish_span(NSPAN - 1)

        phase_ctx2.close()
        for c in range(NC):
            for sp in range(NSPAN):
                dma("sp", lambda e, c=c, sp=sp, s_i=s_i: e.dma_start(
                    x1d_h[:, s_i * NC * S + c * S + sp * SPAN:
                          s_i * NC * S + c * S + sp * SPAN + SPAN],
                    x1ap(c, sp * SPAN, SPAN)),
                    reads=[x1k(c, sp * SPAN)], writes=[K("x1d", s_i, c, sp)],
                    tag="x1sp")

    for s_i in range(NSEQ):
        seq_run(s_i)
    seq_ctx.close()

    PPF.extend([(wfa, K("wfa")), (wfb, K("wfb"))])
    PPB.extend([(wba, K("wba")), (wbb, K("wbb"))])
    PPS[:] = [(wbc, K("wbc")), (wbd, K("wbd")), (wqB, K("wqB"))]
    pump(10 ** 9)
    ffn_ctx = ExitStack()
    sch.fence("dve", sch.keys_with_names(
        ["x1", "q", "kr", "lat", "v65", "ao", "pj", "zst", "qsl", "pp", "xln",
         "wta", "wtb", "wpa", "wpb", "wqA", "wqB", "wfa", "wfb", "wba", "wbb",
         "wbc", "wbd", "psB2"]),
        ([K("x1l", p, c) for p in range(2) for c in range(NC)] +
         [K("aq", c) for c in range(NC)] +
         [K("h", og) for og in range(NCI)] + [K("y", og) for og in range(NC)] +
         [K("gscr", j) for j in range(4)]), tag="fence_ffn")
    x1ld = sb("x1ld", [128, NC * SPAN], BF16, ffn_ctx)
    x1ldB = sb("x1ldB", [128, NC * SPAN], BF16, ffn_ctx)
    aq = sb("aq", [128, NC * SPAN], BF16, ffn_ctx)
    hsp = sb("hsp", [128, NCI * SPAN], BF16, ffn_ctx)
    ysp = sb("ysp", [128, NC * SPAN], BF16, ffn_ctx)
    gscr = sb("gscr", [128, 4 * 512], BF16, ffn_ctx)
    rA2 = sb("rA2", [1, 512], F32, ffn_ctx)
    rB2 = sb("rB2", [1, 512], F32, ffn_ctx)
    mr2 = sb("mr2", [1, 512], F32, ffn_ctx)
    rbA2 = sb("rbA2", [1, 512], BF16, ffn_ctx)
    rbB2 = sb("rbB2", [1, 512], BF16, ffn_ctx)
    bcA2 = sb("bcA2", [128, 512], BF16, ffn_ctx)
    bcB2 = sb("bcB2", [128, 512], BF16, ffn_ctx)
    xst2 = sb("xst2", [128, 512], BF16, ffn_ctx)
    syb = sb("syb", [128, 512], BF16, ffn_ctx)
    KIT1 = Kit(psS=psQ2, psSk=K("psQ2"), psQ=psQ2, psQk=K("psQ2"), single=True,
               r1=rA2, r1k=K("rA2"), r2=rB2, r2k=K("rB2"),
               rb1=rbA2, rb1k=K("rbA2"), rb2=rbB2, rb2k=K("rbB2"),
               bcA=bcA2, bcAk=K("bcA2"), bcB=bcB2, bcBk=K("bcB2"),
               xs=xst2, xsk=K("xst2"), m2=mr2, m2k=K("mr2"))

    def quant_scales(src, nchunks, src_keys, need_sr, uid, sr_cols=None,
                     sr_dst=None):
        view = src.rearrange("p (c t) -> p t c", c=nchunks)
        dve(lambda v: v.tensor_reduce(xstgb[:, 0:SPAN], view, AX.X, ALU.max,
                                      apply_absolute_value=True),
            reads=src_keys + [K("xstgb")], writes=[K("xstgb")], tag=f"{uid}mx")
        nb = SPAN // 128
        for b in range(nb):
            pe(lambda t, b=b: t.transpose(
                psB[:, 128 * b:128 * b + 128], xstgb[:, 128 * b:128 * b + 128],
                cId[:, :]),
               reads=[K("xstgb"), K("cId")], writes=[K("psB")], tag=f"{uid}tr")
        for b in range(nb):
            dve(lambda v, b=b: v.tensor_reduce(
                scc[:, 3 + b:4 + b], psB[:, 128 * b:128 * b + 128], AX.X, ALU.max),
                reads=[K("psB"), K("smax")], writes=[K("smax")], tag=f"{uid}rm")
        dve(lambda v, nb=nb: v.tensor_scalar(
            scc[:, 3:3 + nb], scc[:, 3:3 + nb], 1e-5, None, ALU.max),
            reads=[K("smax")], writes=[K("smax")], tag=f"{uid}cl")
        dve(lambda v, nb=nb: v.reciprocal(scc[:, 8:8 + nb], scc[:, 3:3 + nb]),
            reads=[K("smax"), K("srec")], writes=[K("srec")], tag=f"{uid}rc")
        dve(lambda v, nb=nb: v.tensor_scalar(
            scc[:, 8:8 + nb], scc[:, 8:8 + nb], 127.0, None, ALU.mult),
            reads=[K("srec")], writes=[K("srec")], tag=f"{uid}s")

        for b in range(nb):
            dma("sp", lambda e, b=b: e.dma_start(
                r1[0:1, 128 * b:128 * b + 128], scc[:, 8 + b:9 + b]),
                reads=[K("srec"), K("r1")], writes=[K("r1")], tag=f"{uid}br")
        pe(lambda t: t.matmul(psM[:, 0:SPAN], onesrow_f[:, :], r1[0:1, 0:SPAN],
                              start=True, stop=True),
           reads=[K("onesrow_f"), K("r1")], writes=[K("psM")], tag=f"{uid}bm")
        act(lambda a: a.activation(bcA[:, 0:SPAN], psM[:, 0:SPAN], AF.Copy),
            reads=[K("psM"), K("bcA")], writes=[K("bcA")], tag=f"{uid}be")
        if need_sr:
            if sr_cols is None:
                dve(lambda v, nb=nb: v.tensor_scalar(
                    scc[:, 12:12 + nb], scc[:, 3:3 + nb], 1.0 / 127.0, None,
                    ALU.mult),
                    reads=[K("smax"), K("sr")], writes=[K("sr")], tag=f"{uid}sr")
            else:
                c1, c2, imm = sr_cols
                dve(lambda v, nb=nb, c1=c1, c2=c2: v.tensor_scalar(
                    scc[:, 12:12 + nb], scc[:, 3:3 + nb], c1, c2, ALU.mult,
                    ALU.mult),
                    reads=[K("smax"), K("sr"), K("thr")], writes=[K("sr")],
                    tag=f"{uid}sra")
                dve(lambda v, nb=nb, imm=imm: v.tensor_scalar(
                    scc[:, 12:12 + nb], scc[:, 12:12 + nb], imm, None, ALU.mult),
                    reads=[K("sr")], writes=[K("sr")], tag=f"{uid}srb")

            for b in range(nb):
                dma("sp", lambda e, b=b: e.dma_start(
                    r2[0:1, 128 * b:128 * b + 128], scc[:, 12 + b:13 + b]),
                    reads=[K("sr"), K("r2")], writes=[K("r2")], tag=f"{uid}b2")
            sdst, sdk = (bcB, K("bcB")) if sr_dst is None else sr_dst
            pe(lambda t: t.matmul(psM[:, 0:SPAN], onesrow_f[:, :],
                                  r2[0:1, 0:SPAN], start=True, stop=True),
               reads=[K("onesrow_f"), K("r2")], writes=[K("psM")], tag=f"{uid}m2")
            act(lambda a, sdst=sdst: a.activation(sdst[:, 0:SPAN],
                                                  psM[:, 0:SPAN], AF.Copy),
                reads=[K("psM"), sdk], writes=[sdk], tag=f"{uid}e2")

    def round_gadget(cap, ckey, sr_tile, sr_key=("bcB",), dbg=None):
        dve(lambda v: v.tensor_tensor(gA[:, 0:SPAN], cap, bcA[:, 0:SPAN],
                                      ALU.mult),
            reads=[ckey, K("bcA"), K("wta")], writes=[K("wta")], tag="rg1")
        dve(lambda v: v.tensor_scalar(scrI[:, 0:SPAN], gA[:, 0:SPAN], KOFF,
                                      None, ALU.add),
            reads=[K("wta"), K("scrI")], writes=[K("scrI")], tag="rg2")
        dve(lambda v: v.tensor_copy(xstgb[:, 0:SPAN], scrI[:, 0:SPAN]),
            reads=[K("scrI"), K("xstgb")], writes=[K("xstgb")], tag="rg2b")
        dve(lambda v: v.scalar_tensor_tensor(
            scrA[:, 0:SPAN], gA[:, 0:SPAN], KOFF, xstgb[:, 0:SPAN], ALU.add,
            ALU.subtract),
            reads=[K("wta"), K("xstgb"), K("scrA")], writes=[K("scrA")], tag="rg3")
        if dbg is not None:
            d1, d2, d3 = dbg
            dma("pool", lambda e: e.dma_start(d1[:, 0:SPAN], scrA[:, 0:SPAN]),
                reads=[K("scrA")], writes=[K("dbgD")], tag="dbgD")
            dma("pool", lambda e: e.dma_start(d2[:, 0:SPAN], xstgb[:, 0:SPAN]),
                reads=[K("xstgb")], writes=[K("dbgE")], tag="dbgE")
            dma("pool", lambda e: e.dma_start(d3[:, 0:SPAN], wtb[:, 0:SPAN]),
                reads=[K("wtb")], writes=[K("dbgF")], tag="dbgF")
        dve(lambda v: v.tensor_scalar(gB[:, 0:SPAN], scrA[:, 0:SPAN], 0.5,
                                      None, ALU.is_ge),
            reads=[K("scrA"), K("wtb")], writes=[K("wtb")], tag="rg4")
        dve(lambda v: v.scalar_tensor_tensor(
            gB[:, 0:SPAN], scrA[:, 0:SPAN], -0.5, gB[:, 0:SPAN],
            ALU.is_gt, ALU.add),
            reads=[K("scrA"), K("wtb")], writes=[K("wtb")], tag="rg5")
        if sr_tile is not None:
            dve(lambda v: v.scalar_tensor_tensor(
                gB[:, 0:SPAN], gB[:, 0:SPAN], -(1.0 + KOFF), xstgb[:, 0:SPAN],
                ALU.add, ALU.add),
                reads=[K("wtb"), K("xstgb")], writes=[K("wtb")],
                tag="rg6")
            dve(lambda v: v.tensor_tensor(cap, gB[:, 0:SPAN], sr_tile,
                                          ALU.mult),
                reads=[K("wtb"), sr_key, ckey], writes=[ckey], tag="rg7")
        else:
            dve(lambda v: v.scalar_tensor_tensor(
                cap, gB[:, 0:SPAN], -1.0, xstgb[:, 0:SPAN], ALU.add, ALU.add),
                reads=[K("wtb"), K("xstgb"), ckey], writes=[ckey], tag="rg6b")

    def x1buf(fs):
        return (x1ld, "x1ld") if fs % 2 == 0 else (x1ldB, "x1ldB")

    def ffn_x1load(fs):
        ts0 = fs * SPAN
        s_i = ts0 // S
        lt0 = ts0 % S
        xb, xbn = x1buf(fs)
        par = fs % 2
        for c in range(NC):
            dma("sp", lambda e, c=c, s_i=s_i, lt0=lt0, xb=xb: e.dma_start(
                xb[:, c * SPAN:(c + 1) * SPAN],
                x1d_h[:, s_i * NC * S + c * S + lt0:
                      s_i * NC * S + c * S + lt0 + SPAN]),
                reads=[K("x1d", s_i, c, lt0 // SPAN), K("x1l", par, c)],
                writes=[K("x1l", par, c)], tag="x1ld")

    def ffn_head_stats(fs):
        xb, xbn = x1buf(fs)
        par = fs % 2
        ln_stats_a(lambda c, xb=xb: xb[:, c * SPAN:(c + 1) * SPAN],
                   lambda c, par=par: [K("x1l", par, c)], f"l2_{fs}_",
                   kt=KIT1)

    def ffn_head(fs):
        ts0 = fs * SPAN
        xb, xbn = x1buf(fs)
        par = fs % 2
        ln_stats_b(f"l2_{fs}_", kt=KIT1)
        for c in range(NC):
            dve(lambda v, c=c, xb=xb: v.tensor_tensor(
                aq[:, c * SPAN:(c + 1) * SPAN],
                xb[:, c * SPAN:(c + 1) * SPAN], bcA2[:, 0:SPAN], ALU.mult),
                reads=[K("x1l", par, c), K("bcA2"), K("aq", c)],
                writes=[K("aq", c)], tag="l2a")
            dve(lambda v, c=c: v.tensor_tensor(
                aq[:, c * SPAN:(c + 1) * SPAN], aq[:, c * SPAN:(c + 1) * SPAN],
                bcB2[:, 0:SPAN], ALU.subtract),
                reads=[K("aq", c), K("bcB2")], writes=[K("aq", c)], tag="l2b")
        quant_scales(aq[:, 0:NC * SPAN], NC, [K("aq", c) for c in range(NC)],
                     True, f"qx{fs}")
        if DBG and fs == 0:
            dma("sp", lambda e: e.dma_start(dbgA_h[:, 0:SPAN], bcA[:, 0:SPAN]),
                reads=[K("bcA")], writes=[K("dbgA")], tag="dbgA")
            dma("sp", lambda e: e.dma_start(dbgB_h[:, 0:SPAN], bcB[:, 0:SPAN]),
                reads=[K("bcB")], writes=[K("dbgB")], tag="dbgB")
        for c in range(NC):
            round_gadget(aq[:, c * SPAN:(c + 1) * SPAN], K("aq", c),
                         bcB[:, 0:SPAN],
                         dbg=(dbgD_h, dbgE_h, dbgF_h)
                         if (DBG and fs == 0 and c == 0) else None)
        if DBG and fs == 0:
            dma("sp", lambda e: e.dma_start(dbgC_h[:, 0:SPAN],
                                            aq[:, 0:SPAN]),
                reads=[K("aq", 0)], writes=[K("dbgC")], tag="dbgC")

    def ffn_gateup(fs):
        for og4 in range(NCI // 4):
            for ci in range(NC):
                wb, wk = wbuf_b()
                dma("sp", lambda e, og4=og4, ci=ci, wb=wb:
                    e.dma_start(wb[:, :], wgt_h[128 * ci:128 * ci + 128,
                                                512 * og4:512 * og4 + 512]),
                    reads=[wk] + wq_strip_keys("wg", ci, og4), writes=[wk],
                    tag="gwl")
                for jj in range(4):
                    pt, poff, pk = PS4[jj]
                    pe(lambda t, ci=ci, jj=jj, wb=wb, pt=pt, poff=poff:
                       t.matmul(pt[0:128, poff:poff + SPAN],
                                wb[:, 128 * jj:128 * jj + 128],
                                aq[:, ci * SPAN:(ci + 1) * SPAN],
                                start=(ci == 0), stop=(ci == NC - 1)),
                       reads=[wk, K("aq", ci)], writes=[pk], tag="gmm")
            for jj in range(4):
                pt, poff, pk = PS4[jj]
                act(lambda s, jj=jj, pt=pt, poff=poff: s.activation(
                    gscr[:, jj * 512:jj * 512 + SPAN],
                    pt[0:128, poff:poff + SPAN], AF.Silu, scale=mgcol[:, 0:1]),
                    reads=[pk, K("mgcol"), K("gscr", jj)],
                    writes=[K("gscr", jj)], tag="sil")
            for ci in range(NC):
                wb, wk = wbuf_b()
                dma("sp", lambda e, og4=og4, ci=ci, wb=wb:
                    e.dma_start(wb[:, :], wut_h[128 * ci:128 * ci + 128,
                                                512 * og4:512 * og4 + 512]),
                    reads=[wk] + wq_strip_keys("wu", ci, og4), writes=[wk],
                    tag="uwl")
                for jj in range(4):
                    pt, poff, pk = PS4[jj]
                    pe(lambda t, ci=ci, jj=jj, wb=wb, pt=pt, poff=poff:
                       t.matmul(pt[0:128, poff:poff + SPAN],
                                wb[:, 128 * jj:128 * jj + 128],
                                aq[:, ci * SPAN:(ci + 1) * SPAN],
                                start=(ci == 0), stop=(ci == NC - 1)),
                       reads=[wk, K("aq", ci)], writes=[pk], tag="umm")
            for jj in range(4):
                og = og4 * 4 + jj
                pt, poff, pk = PS4[jj]
                dve(lambda v, og=og, jj=jj, pt=pt, poff=poff: v.tensor_tensor(
                    hsp[:, og * SPAN:(og + 1) * SPAN],
                    pt[0:128, poff:poff + SPAN],
                    gscr[:, jj * 512:jj * 512 + SPAN], ALU.mult),
                    reads=[pk, K("gscr", jj), K("h", og)], writes=[K("h", og)],
                    tag="hm")
    def ffn_hquant(fs):
        quant_scales(hsp[:, 0:NCI * SPAN], NCI,
                     [K("h", og) for og in range(NCI)], True, f"qh{fs}",
                     sr_cols=(thrcol[:, 2:3], thrcol[:, 4:5], 4.0 / 127.0),
                     sr_dst=(syb, K("syb")))
        for c in range(NCI):
            round_gadget(hsp[:, c * SPAN:(c + 1) * SPAN], K("h", c), None)

    def ffn_down(fs):
        for og4 in range(NC // 4):
            for ci in range(NCI):
                wb, wk = wbuf_b()
                dma("sp", lambda e, og4=og4, ci=ci, wb=wb:
                    e.dma_start(wb[:, :], wdt_h[128 * ci:128 * ci + 128,
                                                512 * og4:512 * og4 + 512]),
                    reads=[wk] + wq_strip_keys("wd", ci, og4), writes=[wk],
                    tag="dwl")
                for jj in range(4):
                    pt, poff, pk = PS4[jj]
                    pe(lambda t, ci=ci, jj=jj, wb=wb, pt=pt, poff=poff:
                       t.matmul(pt[0:128, poff:poff + SPAN],
                                wb[:, 128 * jj:128 * jj + 128],
                                hsp[:, ci * SPAN:(ci + 1) * SPAN],
                                start=(ci == 0), stop=(ci == NCI - 1)),
                       reads=[wk, K("h", ci)], writes=[pk], tag="dmm")
            for jj in range(4):
                og = og4 * 4 + jj
                pt, poff, pk = PS4[jj]
                dve(lambda v, og=og, pt=pt, poff=poff: v.scalar_tensor_tensor(
                    ysp[:, og * SPAN:(og + 1) * SPAN],
                    pt[0:128, poff:poff + SPAN], colsumK[:, og:og + 1],
                    syb[:, 0:SPAN], ALU.subtract, ALU.mult),
                    reads=[pk, K("csK"), K("syb"), K("y", og)],
                    writes=[K("y", og)], tag="dev")

    def ffn_tail(fs):
        ts0 = fs * SPAN
        xb, xbn = x1buf(fs)
        par = fs % 2
        ln_stats(lambda c: ysp[:, c * SPAN:(c + 1) * SPAN],
                 lambda c: [K("y", c)], f"l2b_{fs}_", kt=KIT1)
        for c in range(NC):
            dve(lambda v, c=c: v.tensor_tensor(
                ysp[:, c * SPAN:(c + 1) * SPAN],
                ysp[:, c * SPAN:(c + 1) * SPAN], bcA2[:, 0:SPAN], ALU.mult),
                reads=[K("y", c), K("bcA2")], writes=[K("y", c)], tag="ba")
            dve(lambda v, c=c: v.tensor_tensor(
                ysp[:, c * SPAN:(c + 1) * SPAN],
                ysp[:, c * SPAN:(c + 1) * SPAN], bcB2[:, 0:SPAN], ALU.subtract),
                reads=[K("y", c), K("bcB2")], writes=[K("y", c)], tag="bb")
            dve(lambda v, c=c, xb=xb: v.tensor_tensor(
                ysp[:, c * SPAN:(c + 1) * SPAN],
                ysp[:, c * SPAN:(c + 1) * SPAN],
                xb[:, c * SPAN:(c + 1) * SPAN], ALU.add),
                reads=[K("y", c), K("x1l", par, c)], writes=[K("y", c)],
                tag="rs2")
        for b in range(SPAN // 128):
            tg = ts0 + b * 128
            for g in range(2):
                for bb in range(4):
                    c = 4 * g + bb
                    pe(lambda t, c=c, b=b, bb=bb: t.transpose(
                        psB[:, 512 + 128 * bb:512 + 128 * bb + 128],
                        ysp[:, c * SPAN + 128 * b: c * SPAN + 128 * b + 128],
                        cId[:, :]),
                       reads=[K("y", c), K("cId")], writes=[K("psB2")],
                       tag="otr")
                act(lambda a: a.activation(xstage[:, 0:512], psB[:, 512:1024],
                                           AF.Copy),
                    reads=[K("psB2"), K("xstage")], writes=[K("xstage")],
                    tag="oev")
                dma("sp", lambda e, tg=tg, g=g: e.dma_start(
                    out_h[tg:tg + 128, 512 * g:512 * g + 512], xstage[:, 0:512]),
                    reads=[K("xstage")], writes=[K("outd", tg, g)], tag="ost")

    def ffn_run():
        ffn_x1load(0)
        ffn_head_stats(0)
        ffn_head(0)
        ffn_gateup(0)
        for fs in range(NFS):
            if fs + 1 < NFS:
                ffn_x1load(fs + 1)
                ffn_head_stats(fs + 1)
            ffn_hquant(fs)
            ffn_down(fs)
            if fs + 1 < NFS:
                ffn_head(fs + 1)
                ffn_gateup(fs + 1)
            ffn_tail(fs)

    ffn_run()
    ffn_ctx.close()

    tap_src = {"cD": cD, "cC": cC, "colsumK": colsumK, "thrcol": thrcol}
    for tname, pdim, fdim, tdt in taps:
        src = tap_src.get(tname)
        if src is None:
            continue
        dma("sp", lambda e, tname=tname, src=src, pdim=pdim, fdim=fdim:
            e.dma_start(tap_h[tname][:, :], src[0:pdim, 0:fdim]),
            reads=[], writes=[K("tap", tname)], tag="tap")

    outkeys = [K("outd", tg, g) for tg in range(0, T, 128) for g in range(2)]
    sch.op("sp", lambda e: e.nop(), reads=outkeys, writes=[], tag="fin")

    with ctx:
        with ExitStack() as sctx:
            for sname in sch.sem_names:
                sems[sname] = sctx.enter_context(nc.semaphore("sem_" + sname))
            with nc.Block() as block:
                sch.emit(block, sems)
    return nc, sch


DEFAULT_CFG = {"NSEQ": 2, "S": 2048, "D": 1024, "DL": 256, "INNER": 4096, "H": 16}


def kernel(**inputs):
    cfg = DEFAULT_CFG
    x = np.ascontiguousarray(np.asarray(inputs["L0_attn_x"], dtype=np.float32))
    B, S, D = x.shape
    ncores = 8
    seq_per = B // ncores
    wmap = {
        "W_q": np.ascontiguousarray(np.asarray(inputs["L0_attn_W_q"], np.float32)),
        "W_k": np.ascontiguousarray(np.asarray(inputs["L0_attn_W_k"], np.float32)),
        "W_v": np.ascontiguousarray(np.asarray(inputs["L0_attn_W_v"], np.float32)),
        "W_o": np.ascontiguousarray(np.asarray(inputs["L0_attn_W_o"], np.float32)),
        "W_down_kv": np.ascontiguousarray(
            np.asarray(inputs["L0_attn_W_down_kv"], np.float32)),
        "W_up_k": np.ascontiguousarray(np.asarray(inputs["L0_attn_W_up_k"], np.float32)),
        "W_up_v": np.ascontiguousarray(np.asarray(inputs["L0_attn_W_up_v"], np.float32)),
        "w_gate": np.ascontiguousarray(np.asarray(inputs["L0_ffn_w_gate"], np.float32)),
        "w_up": np.ascontiguousarray(np.asarray(inputs["L0_ffn_w_up"], np.float32)),
        "w_down": np.ascontiguousarray(np.asarray(inputs["L0_ffn_w_down"], np.float32)),
    }
    nc, _ = build_program(cfg)
    in_maps = []
    for c in range(ncores):
        m = dict(wmap)
        m["x"] = np.ascontiguousarray(
            x[c * seq_per:(c + 1) * seq_per].reshape(seq_per * S, D))
        in_maps.append(m)
    res = run_bass_kernel_spmd(nc, in_maps, core_ids=list(range(ncores)))
    out = np.concatenate(
        [r["out"].reshape(seq_per, S, D) for r in res.results], axis=0)
    return np.ascontiguousarray(out.astype(np.float32))
